# version 02
# baseline (speedup 1.0000x reference)
import math
import numpy as np
import ml_dtypes
from contextlib import ExitStack
from types import SimpleNamespace

import concourse.bass as bass
import concourse.mybir as mybir
from concourse.bass_utils import run_bass_kernel_spmd

F32 = mybir.dt.float32
BF16 = mybir.dt.bfloat16
I32 = mybir.dt.int32
AF = mybir.ActivationFunctionType
ALU = mybir.AluOpType
AX = mybir.AxisListType

EPS = 1e-5
QSC = 1.0 / math.sqrt(128.0)


def make_cfg(D, I, S):
    c = SimpleNamespace()
    c.D, c.I, c.S = D, I, S
    c.C = D // 128
    c.IC = I // 128
    c.ICH = c.IC // 2
    c.TPC = S // 4
    c.TH = c.TPC // 2
    c.H = D // 128
    c.HPC = c.H // 4
    c.PAIRS = c.HPC // 2
    c.NQ = S // 128
    c.TGN = S // c.TH
    c.CG = min(8, c.C)
    c.CGN = c.C // c.CG
    return c


FULL = make_cfg(4096, 11008, 4096)


class Ctr:

    def __init__(self, nc, ctx, name):
        self.sem = ctx.enter_context(nc.semaphore(name))
        self.n = 0

    def tag(self):
        self.n += 1
        return self.n


class Chan:

    def __init__(self, nc, ctx, name, nslots):
        self.sems = [ctx.enter_context(nc.semaphore(f"{name}{i}")) for i in range(nslots)]
        self.cnt = [0] * nslots

    def tag(self, slot):
        self.cnt[slot] += 1
        return (self.sems[slot], 16 * self.cnt[slot])

    def total(self, slot=0):
        return (self.sems[slot], 16 * self.cnt[slot])


class Sched:
    ENGINES = ("pe", "dve", "act", "sp", "gp")

    def __init__(self):
        self.q = {e: [] for e in self.ENGINES}

    def add(self, eng, fn):
        self.q[eng].append(fn)

    def wait(self, eng, sem, val):
        if val is None:
            return
        self.q[eng].append(lambda E, s=sem, v=val: E.wait_ge(s, v))

    def wait_tok(self, eng, tok):
        if tok is None:
            return
        sem, val = tok
        self.wait(eng, sem, val)


def run_block(nc, S):
    with nc.Block() as blk:
        if S.q["pe"]:
            @blk.tensor
            def _(E):
                for fn in S.q["pe"]:
                    fn(E)
        if S.q["dve"]:
            @blk.vector
            def _(E):
                for fn in S.q["dve"]:
                    fn(E)
        if S.q["act"]:
            @blk.scalar
            def _(E):
                for fn in S.q["act"]:
                    fn(E)
        if S.q["sp"]:
            @blk.sync
            def _(E):
                for fn in S.q["sp"]:
                    fn(E)
        if S.q["gp"]:
            @blk.gpsimd
            def _(E):
                for fn in S.q["gp"]:
                    fn(E)



def emit_init(S, R):
    sb, dr = R.sb, R.dr
    ch = R.ch_const
    pairs = [
        (sb.ident, dr.ident), (sb.mask01, dr.mask01),
        (sb.bq, dr.bq_d), (sb.bproj, dr.bproj_d),
        (sb.scl_row, dr.scl), (sb.blkoff, dr.blkoff),
        (sb.cvals, dr.cvals),
    ]
    for dst, src in pairs:
        ch.tag(0)
        S.add("gp", lambda E, d=dst, s=src, m=ch.sems[0]:
              E.dma_start(out=d[:, :], in_=s[:, :]).then_inc(m, 16))
    total = ch.total(0)

    S.add("dve", lambda E: E.memset(R.sb.ones_col[:, :], 1.0))
    t_ones = R.c_dve.tag()
    S.add("dve", lambda E, m=R.c_dve.sem:
          E.memset(R.sb.ones_row[:, :], 1.0).then_inc(m, 1))

    S.wait_tok("pe", total)
    S.wait("pe", R.c_dve.sem, t_ones)
    t_bc = R.c_pe.tag()
    S.add("pe", lambda E, m=R.c_pe.sem: E.matmul(
        R.ps0[:, 0:3], R.sb.ones_row[:, :], R.sb.scl_row[0:1, :],
        start=True, stop=True).then_inc(m, 1))
    S.wait("act", R.c_pe.sem, t_bc)
    t_ev = R.c_act.tag()
    S.add("act", lambda E, m=R.c_act.sem:
          E.copy(R.sb.scl_b[:, 0:3], R.ps0[:, 0:3]).then_inc(m, 1))
    S.wait("dve", R.c_act.sem, t_ev)
    t_ds = R.c_dve.tag()
    S.add("dve", lambda E, m=R.c_dve.sem: E.tensor_scalar(
        R.sb.scl_b[:, 3:4], R.sb.scl_b[:, 2:3], 0.5, None, ALU.mult).then_inc(m, 1))
    S.wait_tok("gp", total)
    S.wait("gp", R.c_dve.sem, t_ds)


def emit_ln_rows_and_bcast(S, R, p_sum, p_sq, stats_toks, dst_ab, scr_rows=0, defer=False):
    cfg = R.cfg
    A, B = dst_ab
    r = R.sb.rows
    S.wait("dve", R.c_pe.sem, stats_toks[0])
    S.wait("dve", R.c_pe.sem, stats_toks[1])
    S.add("dve", lambda E, p=p_sum: E.tensor_scalar(
        r[0][0:1, :], p[0:1, :], 1.0 / cfg.D, None, ALU.mult))
    S.add("dve", lambda E, p=p_sq: E.tensor_scalar(
        r[1][0:1, :], p[0:1, :], 1.0 / cfg.D, None, ALU.mult))
    S.add("dve", lambda E: E.tensor_tensor(r[3][0:1, :], r[0][0:1, :], r[0][0:1, :], ALU.mult))
    t_var = R.c_dve.tag()
    S.add("dve", lambda E, m=R.c_dve.sem: E.tensor_tensor(
        r[2][0:1, :], r[1][0:1, :], r[3][0:1, :], ALU.subtract).then_inc(m, 1))
    S.wait("act", R.c_dve.sem, t_var)
    S.add("act", lambda E: E.activation(
        r[1][0:1, :], r[2][0:1, :], AF.Ln, bias=EPS, scale=1.0))
    t_sqrt = R.c_act.tag()
    S.add("act", lambda E, m=R.c_act.sem: E.activation(
        r[2][0:1, :], r[1][0:1, :], AF.Exp, bias=0.0, scale=-0.5).then_inc(m, 1))
    S.wait("dve", R.c_act.sem, t_sqrt)
    S.add("dve", lambda E: E.tensor_tensor(r[3][0:1, :], r[0][0:1, :], r[2][0:1, :], ALU.mult))
    t_ab = R.c_dve.tag()
    S.add("dve", lambda E, m=R.c_dve.sem: E.tensor_scalar(
        r[3][0:1, :], r[3][0:1, :], -1.0, None, ALU.mult).then_inc(m, 1))
    TH = r[0].shape[1]
    S.wait("sp", R.c_dve.sem, t_ab)
    S.wait("sp", R.c_act.sem, t_sqrt)
    if R.bc_prev is not None:
        for sem_, val_ in R.bc_prev:
            S.wait("sp", sem_, val_)
    row0 = scr_rows
    w1 = R.ch_bcw.tag(0)
    S.add("sp", lambda E, r0=row0, m=w1[0]: E.dma_start(
        out=R.dr.bc_scr[r0:r0 + 1, :], in_=r[2][0:1, :]).then_inc(m, 16))
    w2 = R.ch_bcw.tag(1)
    S.add("sp", lambda E, r0=row0, m=w2[0]: E.dma_start(
        out=R.dr.bc_scr[r0 + 1:r0 + 2, :], in_=r[3][0:1, :]).then_inc(m, 16))
    S.wait_tok("sp", w1)
    S.wait_tok("sp", w2)
    if defer:
        return (w1, w2)
    return emit_ab_load(S, R, row0, A, B)


def emit_ab_load(S, R, row0, A, B):
    TH = A.shape[1]
    if R.ab_free is not None:
        for sem_, val_ in R.ab_free:
            S.wait("sp", sem_, val_)
    rA = R.ch_bcr.tag(0)
    S.add("sp", lambda E, d=A, r0=row0, m=rA[0]: E.dma_start(
        out=d[:, :], in_=bass.AP(R.dr.bc_scr_t, r0 * TH, [[0, 128], [1, TH]])).then_inc(m, 16))
    rB = R.ch_bcr.tag(1)
    S.add("sp", lambda E, d=B, r0=row0, m=rB[0]: E.dma_start(
        out=d[:, :], in_=bass.AP(R.dr.bc_scr_t, (r0 + 1) * TH, [[0, 128], [1, TH]])).then_inc(m, 16))
    R.bc_prev = [rA, rB]
    return (rA, rB)


def emit_ln_stats_stream(S, R, src_dram, half, bank_guard=(None, None),
                         store_toks=None, prev_waits=(), sq_prev=(), psums=None):
    cfg = R.cfg
    p_sum, p_sq = psums if psums is not None else (R.ps2, R.ps3)
    xin, sq = R.sb.xin, R.sb.sq
    pe_x = [None] * cfg.C
    act_x = [None] * cfg.C
    pe_q = [None] * cfg.C
    for c in range(cfg.C):
        sl = c % 2
        if c == 0:
            for sem_, val_ in prev_waits:
                S.wait("sp", sem_, val_)
        if store_toks is not None:
            S.wait_tok("sp", store_toks[c])
        if c >= 2:
            S.wait("sp", R.c_pe.sem, pe_x[c - 2])
            S.wait("sp", R.c_act.sem, act_x[c - 2])
        tok = R.ch_x.tag(sl)
        src_ap = src_dram[c * 128:(c + 1) * 128, half * cfg.TH:(half + 1) * cfg.TH]
        S.add("sp", lambda E, d=xin[sl], s=src_ap, m=tok[0]:
              E.dma_start(out=d[:, :], in_=s).then_inc(m, 16))
        S.wait_tok("act", tok)
        if c < 2:
            for sem_, val_ in sq_prev:
                S.wait("act", sem_, val_)
        if c >= 2:
            S.wait("act", R.c_pe.sem, pe_q[c - 2])
        act_x[c] = R.c_act.tag()
        S.add("act", lambda E, d=sq[sl], s=xin[sl], m=R.c_act.sem:
              E.activation(d[:, :], s[:, :], AF.Square).then_inc(m, 1))
        S.wait_tok("pe", tok)
        if c == 0:
            S.wait("pe", R.c_act.sem, bank_guard[0])
        pe_x[c] = R.c_pe.tag()
        S.add("pe", lambda E, s=xin[sl], p=p_sum, m=R.c_pe.sem, st=(c == 0), sp_=(c == cfg.C - 1):
              E.matmul(p[0:1, :], R.sb.ones_col[:, :], s[:, :],
                       start=st, stop=sp_).then_inc(m, 1))
        S.wait("pe", R.c_act.sem, act_x[c])
        if c == 0:
            S.wait("pe", R.c_act.sem, bank_guard[1])
        pe_q[c] = R.c_pe.tag()
        S.add("pe", lambda E, s=sq[sl], p=p_sq, m=R.c_pe.sem, st=(c == 0), sp_=(c == cfg.C - 1):
              E.matmul(p[0:1, :], R.sb.ones_col[:, :], s[:, :],
                       start=st, stop=sp_).then_inc(m, 1))
    xin_free = [(R.c_pe.sem, pe_x[cfg.C - 2]), (R.c_pe.sem, pe_x[cfg.C - 1]),
                (R.c_act.sem, act_x[cfg.C - 2]), (R.c_act.sem, act_x[cfg.C - 1])]
    sq_free = [(R.c_pe.sem, pe_q[cfg.C - 2]), (R.c_pe.sem, pe_q[cfg.C - 1])]
    return (pe_x[cfg.C - 1], pe_q[cfg.C - 1]), xin_free, sq_free


def emit_normalize_stream(S, R, src_dram, half, ab, ab_ready, store_toks=None,
                          prev_waits=()):
    cfg = R.cfg
    A, B = ab
    xin = R.sb.xin
    dve_n = [None] * cfg.C
    for c in range(cfg.C):
        sl = c % 2
        if c == 0:
            for sem_, val_ in prev_waits:
                S.wait("sp", sem_, val_)
        if store_toks is not None:
            S.wait_tok("sp", store_toks[c])
        if c >= 2:
            S.wait("sp", R.c_dve.sem, dve_n[c - 2])
        tok = R.ch_x.tag(sl)
        src_ap = src_dram[c * 128:(c + 1) * 128, half * cfg.TH:(half + 1) * cfg.TH]
        S.add("sp", lambda E, d=xin[sl], s=src_ap, m=tok[0]:
              E.dma_start(out=d[:, :], in_=s).then_inc(m, 16))
        S.wait_tok("dve", tok)
        if c == 0:
            for sem_, val_ in ab_ready:
                S.wait("dve", sem_, val_)
        S.add("dve", lambda E, s=xin[sl], a=A:
              E.tensor_tensor(R.sb.tmp[:, :], s[:, :], a[:, :], ALU.mult))
        dve_n[c] = R.c_dve.tag()
        S.add("dve", lambda E, c_=c, b=B, m=R.c_dve.sem: E.tensor_tensor(
            R.sb.y[:, c_, :], R.sb.tmp[:, :], b[:, :], ALU.add).then_inc(m, 1))
    R.ab_free = [(R.c_dve.sem, dve_n[cfg.C - 1])]
    return dve_n[cfg.C - 1], [(R.c_dve.sem, dve_n[cfg.C - 2]),
                              (R.c_dve.sem, dve_n[cfg.C - 1])]


def emit_debug_dump(S, R, sbuf_src, dram_dst, gate_ctr, gate_tok):
    S.wait(("sp"), gate_ctr.sem, gate_tok)
    tok = R.ch_dbg.tag(0)
    S.add("sp", lambda E, d=dram_dst, s=sbuf_src, m=tok[0]:
          E.dma_start(out=d, in_=s).then_inc(m, 16))
    S.wait_tok("sp", tok)
    S.wait_tok("dve", tok)
    S.wait_tok("act", tok)


def emit_ffn(S, R, app, half, y_ready, xin_prev=(), sq_prev=()):
    cfg = R.cfg
    sb = R.sb
    y, h = sb.y, sb.h
    resid_src = R.dr.x_fm if app == 1 else R.dr.x2_save
    out_dst = R.dr.x1_save if app == 1 else R.dr.out_fm

    pg = [R.ps0, R.ps1]
    pu = [R.ps2, R.ps3]
    pe_g = [None] * cfg.IC
    pe_u = [None] * cfg.IC
    act_g = [None] * cfg.IC
    act_u = [None] * cfg.IC
    dve_h = [None] * cfg.IC
    for n in range(cfg.IC):
        sl = n % 2
        if n >= 2:
            S.wait("gp", R.c_pe.sem, pe_g[n - 2])
            S.wait("gp", R.c_pe.sem, pe_u[n - 2])
        tg = R.ch_wg.tag(sl)
        S.add("gp", lambda E, d=sb.wg2[sl], s=R.dr.wg_hb[n, :, :, :], m=tg[0]:
              E.dma_start(out=d[:, :, :], in_=s).then_inc(m, 16))
        tu = R.ch_wu.tag(sl)
        S.add("gp", lambda E, d=sb.wu2[sl], s=R.dr.wu_hb[n, :, :, :], m=tu[0]:
              E.dma_start(out=d[:, :, :], in_=s).then_inc(m, 16))
        S.wait_tok("pe", tg)
        if n == 0:
            S.wait("pe", R.c_dve.sem, y_ready)
        if n >= 2:
            S.wait("pe", R.c_act.sem, act_g[n - 2])
        for c in range(cfg.C):
            if c == cfg.C - 1:
                pe_g[n] = R.c_pe.tag()
                S.add("pe", lambda E, p=pg[sl], w=sb.wg2[sl], c_=c, m=R.c_pe.sem:
                      E.matmul(p[:, :], w[:, c_, :], y[:, c_, :],
                               start=False, stop=True).then_inc(m, 1))
            else:
                S.add("pe", lambda E, p=pg[sl], w=sb.wg2[sl], c_=c, st=(c == 0):
                      E.matmul(p[:, :], w[:, c_, :], y[:, c_, :], start=st, stop=False))
        S.wait_tok("pe", tu)
        if n >= 2:
            S.wait("pe", R.c_act.sem, act_u[n - 2])
        for c in range(cfg.C):
            if c == cfg.C - 1:
                pe_u[n] = R.c_pe.tag()
                S.add("pe", lambda E, p=pu[sl], w=sb.wu2[sl], c_=c, m=R.c_pe.sem:
                      E.matmul(p[:, :], w[:, c_, :], y[:, c_, :],
                               start=False, stop=True).then_inc(m, 1))
            else:
                S.add("pe", lambda E, p=pu[sl], w=sb.wu2[sl], c_=c, st=(c == 0):
                      E.matmul(p[:, :], w[:, c_, :], y[:, c_, :], start=st, stop=False))
        S.wait("act", R.c_pe.sem, pe_g[n])
        act_g[n] = R.c_act.tag()
        S.add("act", lambda E, p=pg[sl], n_=n, m=R.c_act.sem: E.activation(
            h[:, n_, :], p[:, :], AF.Silu, bias=0.0, scale=sb.scl_b[:, 0:1]).then_inc(m, 1))
        S.wait("act", R.c_pe.sem, pe_u[n])
        if n >= 2:
            S.wait("act", R.c_dve.sem, dve_h[n - 2])
        act_u[n] = R.c_act.tag()
        S.add("act", lambda E, p=pu[sl], d=sb.ustage[sl], m=R.c_act.sem: E.activation(
            d[:, :], p[:, :], AF.Copy, bias=0.0, scale=sb.scl_b[:, 1:2]).then_inc(m, 1))
        S.wait("dve", R.c_act.sem, act_g[n])
        S.wait("dve", R.c_act.sem, act_u[n])
        dve_h[n] = R.c_dve.tag()
        S.add("dve", lambda E, n_=n, u=sb.ustage[sl], m=R.c_dve.sem: E.tensor_tensor(
            h[:, n_, :], h[:, n_, :], u[:, :], ALU.mult).then_inc(m, 1))

    pd = [R.ps0, R.ps1]
    xin = R.sb.xin
    pe_d = [None] * cfg.C
    dve_d = [None] * cfg.C
    st_tok = [None] * cfg.C
    wd_cons = [None] * (2 * cfg.C)
    wd_toks = {}

    for m_ in range(cfg.C):
        sl = m_ % 2
        for j in range(2):
            idx = 2 * m_ + j
            if idx >= 2:
                S.wait("gp", R.c_pe.sem, wd_cons[idx - 2])
            td = R.ch_wd.tag(j)
            wd_toks[idx] = td
            S.add("gp", lambda E, d=sb.wd2[j], s=R.dr.wd_hb[m_, :, j * cfg.ICH:(j + 1) * cfg.ICH, :],
                  m=td[0]: E.dma_start(out=d[:, :, :], in_=s).then_inc(m, 16))
        if m_ == 0:
            for sem_, val_ in xin_prev:
                S.wait("sp", sem_, val_)
        if m_ >= 2:
            S.wait("sp", R.c_dve.sem, dve_d[m_ - 2])
        tr = R.ch_x.tag(sl)
        src_ap = resid_src[m_ * 128:(m_ + 1) * 128, half * cfg.TH:(half + 1) * cfg.TH]
        S.add("sp", lambda E, d=xin[sl], s=src_ap, m=tr[0]:
              E.dma_start(out=d[:, :], in_=s).then_inc(m, 16))
        if m_ == 0:
            S.wait("pe", R.c_dve.sem, dve_h[cfg.IC - 1])
            S.wait("pe", R.c_act.sem, act_g[cfg.IC - 1])
            S.wait("pe", R.c_act.sem, act_g[cfg.IC - 2])
        if m_ >= 2:
            S.wait("pe", R.c_dve.sem, dve_d[m_ - 2])
        for j in range(2):
            idx = 2 * m_ + j
            S.wait_tok("pe", wd_toks[idx])
            for nn in range(cfg.ICH):
                n = j * cfg.ICH + nn
                first = n == 0
                last = n == cfg.IC - 1
                grp_last = nn == cfg.ICH - 1
                if grp_last:
                    tokp = R.c_pe.tag()
                    wd_cons[idx] = tokp
                    if last:
                        pe_d[m_] = tokp
                    S.add("pe", lambda E, p=pd[sl], w=sb.wd2[j], nn_=nn, n_=n, st=first,
                          sp_=last, m=R.c_pe.sem: E.matmul(
                        p[:, :], w[:, nn_, :], h[:, n_, :], start=st, stop=sp_).then_inc(m, 1))
                else:
                    S.add("pe", lambda E, p=pd[sl], w=sb.wd2[j], nn_=nn, n_=n, st=first:
                          E.matmul(p[:, :], w[:, nn_, :], h[:, n_, :], start=st, stop=False))
        S.wait("dve", R.c_pe.sem, pe_d[m_])
        S.wait_tok("dve", tr)
        if m_ >= 2:
            S.wait_tok("dve", st_tok[m_ - 2])
        S.add("dve", lambda E, p=pd[sl], d=sb.tmpd[sl]: E.tensor_scalar(
            d[:, :], p[:, :], sb.scl_b[:, 3:4], None, ALU.mult))
        dve_d[m_] = R.c_dve.tag()
        S.add("dve", lambda E, s=xin[sl], d=sb.x1stage[sl], t=sb.tmpd[sl], m=R.c_dve.sem:
              E.tensor_tensor(d[:, :], t[:, :], s[:, :], ALU.add).then_inc(m, 1))
        S.wait("sp", R.c_dve.sem, dve_d[m_])
        stok = R.ch_sx1.tag(sl)
        st_tok[m_] = stok
        dst_ap = out_dst[m_ * 128:(m_ + 1) * 128, half * cfg.TH:(half + 1) * cfg.TH]
        S.add("sp", lambda E, d=dst_ap, s=sb.x1stage[sl], m=stok[0]:
              E.dma_start(out=d, in_=s[:, :]).then_inc(m, 16))

    if app == 1:
        down_xin_free = [(R.c_dve.sem, dve_d[cfg.C - 2]), (R.c_dve.sem, dve_d[cfg.C - 1])]
        sqp = list(sq_prev) + [(R.c_dve.sem, dve_d[cfg.C - 2]), (R.c_dve.sem, dve_d[cfg.C - 1])]
        stats, xin_free2, sq_free2 = emit_ln_stats_stream(
            S, R, R.dr.x1_save, half,
            bank_guard=(act_u[cfg.IC - 2], act_u[cfg.IC - 1]),
            store_toks=st_tok, prev_waits=down_xin_free, sq_prev=sqp)
        R.ffn_sq_free = sq_free2
        R.ab_free = (R.ab_free or []) + [st_tok[cfg.C - 2], st_tok[cfg.C - 1],
                                         (R.c_dve.sem, dve_d[cfg.C - 1])]
        if getattr(R.cfg, "debug", False) and half == 0:
            S.wait("act", R.c_pe.sem, stats[0])
            S.wait("act", R.c_pe.sem, stats[1])
            tdbg = R.c_act.tag()
            S.add("act", lambda E, m=R.c_act.sem: E.copy(
                R.sb.dbg_rows_sb[0:1, :], R.ps2[0:1, :]).then_inc(m, 1))
            S.wait("sp", R.c_act.sem, tdbg)
            tokd = R.ch_dbg.tag(0)
            S.add("sp", lambda E, m=tokd[0]: E.dma_start(
                out=R.dr.dbg_rows[0:1, 0:R.cfg.TH], in_=R.sb.dbg_rows_sb[0:1, :]).then_inc(m, 16))
            S.wait_tok("sp", tokd)
            S.wait("act", R.ch_dbg.sems[0], tokd[1])
            tdbg = R.c_act.tag()
            S.add("act", lambda E, m=R.c_act.sem: E.copy(
                R.sb.dbg_rows_sb[0:1, :], R.ps3[0:1, :]).then_inc(m, 1))
            S.wait("sp", R.c_act.sem, tdbg)
            tokd = R.ch_dbg.tag(0)
            S.add("sp", lambda E, m=tokd[0]: E.dma_start(
                out=R.dr.dbg_rows[0:1, R.cfg.TH:2 * R.cfg.TH], in_=R.sb.dbg_rows_sb[0:1, :]).then_inc(m, 16))
            S.wait_tok("sp", tokd)
            S.wait("pe", R.c_act.sem, tdbg)
            S.wait("dve", R.c_act.sem, tdbg)
        ab_ready = emit_ln_rows_and_bcast(
            S, R, R.ps2, R.ps3, stats, (sb.A, sb.B))
        if getattr(R.cfg, "debug", False) and half == 0:
            S.wait_tok("sp", ab_ready[0])
            S.wait_tok("sp", ab_ready[1])
            tokd = R.ch_dbg.tag(0)
            S.add("sp", lambda E, m=tokd[0]: E.dma_start(
                out=R.dr.dbg_ab[:, 0:cfg.TH], in_=sb.A[:, :]).then_inc(m, 16))
            tokd2 = R.ch_dbg.tag(0)
            S.add("sp", lambda E, m=tokd2[0]: E.dma_start(
                out=R.dr.dbg_ab[:, cfg.TH:2 * cfg.TH], in_=sb.B[:, :]).then_inc(m, 16))
            S.wait_tok("sp", tokd2)
            tokd3 = R.ch_dbg.tag(0)
            S.add("sp", lambda E, m=tokd3[0]: E.dma_start(
                out=R.dr.dbg_rows2[0:1, 0:cfg.TH], in_=R.sb.rows[2][0:1, :]).then_inc(m, 16))
            tokd4 = R.ch_dbg.tag(0)
            S.add("sp", lambda E, m=tokd4[0]: E.dma_start(
                out=R.dr.dbg_rows2[0:1, cfg.TH:2 * cfg.TH], in_=R.sb.rows[3][0:1, :]).then_inc(m, 16))
            S.wait_tok("sp", tokd4)
        last_norm, _ = emit_normalize_stream(
            S, R, R.dr.x1_save, half, (sb.A, sb.B), ab_ready,
            prev_waits=xin_free2)
        S.wait("sp", R.c_dve.sem, last_norm)
        ytok = R.ch_y2w.tag(0)
        dst = R.dr.y2_cc_ap.rearrange("(c p) t -> p c t", p=128)[
            :, :, half * cfg.TH:(half + 1) * cfg.TH]
        S.add("sp", lambda E, d=dst, m=ytok[0]:
              E.dma_start(out=d, in_=sb.y[:, :, :]).then_inc(m, 16))
        R.y2_store_toks.append(ytok)
    else:
        S.wait_tok("sp", st_tok[cfg.C - 2])
        S.wait_tok("sp", st_tok[cfg.C - 1])


def emit_collective(S, R, which):
    if which == 1:
        for t in R.y2_store_toks:
            S.wait_tok("gp", t)
        i_h, o_h = R.dr.y2_cc_h, R.dr.y2_gath_h
    else:
        S.wait_tok("gp", R.ch_ao.total(0))
        i_h, o_h = R.dr.ao_cc_h, R.dr.ao_gath_h
    tcc = R.c_cc.tag()
    S.add("gp", lambda E, i=i_h, o=o_h, m=R.c_cc.sem: E.collective_compute(
        "AllGather", ALU.bypass,
        replica_groups=[[0, 1, 2, 3], [4, 5, 6, 7]],
        ins=[i.ap().opt()], outs=[o.ap().opt()]).then_inc(m, 1))
    S.wait("gp", R.c_cc.sem, tcc)
    if getattr(R.cfg, "debug", False):
        dbg_dst = R.dr.dbg_y2cc if which == 1 else R.dr.dbg_aocc
        tok = R.ch_dbg.tag(0)
        S.add("gp", lambda E, d=dbg_dst, s=o_h.ap(), m=tok[0]:
              E.dma_start(out=d, in_=s).then_inc(m, 16))
        S.wait_tok("gp", tok)
        if which == 2:
            tok2 = R.ch_dbg.tag(0)
            S.add("gp", lambda E, d=R.dr.dbg_aoctr, s=i_h.ap(), m=tok2[0]:
                  E.dma_start(out=d, in_=s).then_inc(m, 16))
            S.wait_tok("gp", tok2)


def emit_attention(S, R):
    cfg = R.cfg
    sb = R.sb
    R.ch_cs.tag(0)
    S.add("sp", lambda E, m=R.ch_cs.sems[0]: E.dma_start(
        out=sb.cos_t[:, :], in_=R.dr.cos_t[:, :]).then_inc(m, 16))
    tc2 = R.ch_cs.tag(0)
    S.add("sp", lambda E, m=tc2[0]: E.dma_start(
        out=sb.sin_t[:, :], in_=R.dr.sin_t[:, :]).then_inc(m, 16))
    S.wait_tok("dve", tc2)
    prev_head_done = None
    for pair in range(cfg.PAIRS):
        hA, hB = 2 * pair, 2 * pair + 1
        half_c = cfg.C // 2
        wq_toks = []
        for wslot in range(2):
            S.wait("gp", R.c_pe.sem, R.wq_free[wslot])
            tok = R.ch_wq.tag(wslot)
            src = R.dr.wqkv[:, pair * 768:(pair + 1) * 768].rearrange(
                "(c p) j -> p c j", p=128)[:, wslot * half_c:(wslot + 1) * half_c, :]
            S.add("gp", lambda E, d=sb.wq[:, wslot * half_c:(wslot + 1) * half_c, :],
                  s=src, m=tok[0]: E.dma_start(out=d, in_=s).then_inc(m, 16))
            wq_toks.append(tok)
        t_bqs = None
        for i, hh in enumerate((hA, hB)):
            t_bqs = R.c_dve.tag()
            S.add("dve", lambda E, i_=i, hh_=hh, m=R.c_dve.sem: E.tensor_scalar(
                sb.bqs[:, i_:i_ + 1], sb.bq[:, 3 * hh_:3 * hh_ + 1], QSC, None,
                ALU.mult).then_inc(m, 1))

        pq = [R.aq[0], R.aq[1]]
        pk = [R.aq[2], R.aq[3]]
        pv = [R.aq[4], R.aq[5]]
        ptv = R.atr[0]
        QX, KX, VX = [sb.Qa, sb.Qb], [sb.Ka, sb.Kb], [sb.Va, sb.Vb]
        rope_last = [None, None]
        vtok_last = [None, None]
        evac_last = [[None, None, None], [None, None, None]]
        for tg in range(cfg.TGN):
            blk_i, colh = tg // 2, tg % 2
            pe_last = {}
            for cg in range(cfg.CGN):
                gidx = (pair * cfg.TGN + tg) * cfg.CGN + cg
                ysl = gidx % 2
                S.wait("sp", R.c_pe.sem, R.y2_cons.get(gidx - 2))
                tok = R.ch_y2r.tag(ysl)
                rows = R.dr.y2_gath[
                    blk_i * cfg.D + cg * cfg.CG * 128:
                    blk_i * cfg.D + (cg + 1) * cfg.CG * 128,
                    colh * cfg.TH:(colh + 1) * cfg.TH]
                src = rows.rearrange("(c p) t -> p c t", p=128)
                S.add("sp", lambda E, d=sb.y2sb[ysl], s=src, m=tok[0]:
                      E.dma_start(out=d[:, :, :], in_=s).then_inc(m, 16))
                S.wait_tok("pe", tok)
                if cg == 0:
                    S.wait_tok("pe", wq_toks[0])
                    if cfg.CGN == 1:
                        S.wait_tok("pe", wq_toks[1])
                    if tg == 0:
                        S.wait("pe", R.c_dve.sem, prev_head_done)
                    for i in range(2):
                        for kk in range(3):
                            S.wait("pe", R.c_act.sem, evac_last[i][kk])
                elif cg == (cfg.CGN + 1) // 2:
                    S.wait_tok("pe", wq_toks[1])
                for c8 in range(cfg.CG):
                    c = cg * cfg.CG + c8
                    first = c == 0
                    last = c == cfg.C - 1
                    cg_final = c8 == cfg.CG - 1
                    for i in range(2):
                        for kk in range(3):
                            pp = (pq, pk, pv)[kk]
                            jcol = i * 384 + kk * 128
                            tag_it = last or (cg_final and i == 1 and kk == 2)
                            if tag_it:
                                tokm = R.c_pe.tag()
                                if last:
                                    pe_last[(i, kk)] = tokm
                                if cg_final and i == 1 and kk == 2:
                                    R.y2_cons[gidx] = tokm
                                S.add("pe", lambda E, p=pp[i], c_=c, j=jcol, ysl_=ysl,
                                      c8_=c8, st=first, sp_=last, m=R.c_pe.sem: E.matmul(
                                    p[:, 0:cfg.TH], sb.wq[:, c_, j:j + 128],
                                    sb.y2sb[ysl_][:, c8_, :],
                                    start=st, stop=sp_).then_inc(m, 1))
                            else:
                                S.add("pe", lambda E, p=pp[i], c_=c, j=jcol, ysl_=ysl,
                                      c8_=c8, st=first: E.matmul(
                                    p[:, 0:cfg.TH], sb.wq[:, c_, j:j + 128],
                                    sb.y2sb[ysl_][:, c8_, :], start=st, stop=False))
            for i in range(2):
                hh = (hA, hB)[i]
                S.wait("act", R.c_pe.sem, pe_last[(i, 0)])
                S.wait("act", R.c_dve.sem, t_bqs)
                S.wait("act", R.c_dve.sem, rope_last[i])
                evac_last[i][0] = R.c_act.tag()
                S.add("act", lambda E, p=pq[i], i_=i, m=R.c_act.sem: E.activation(
                    sb.qf[:, :], p[:, 0:cfg.TH], AF.Identity, bias=sb.bqs[:, i_:i_ + 1],
                    scale=QSC).then_inc(m, 1))
                S.add("act", lambda E: E.copy(sb.rot[0:64, :], sb.qf[64:128, :]))
                t_rot = R.c_act.tag()
                S.add("act", lambda E, m=R.c_act.sem:
                      E.copy(sb.rot[64:128, :], sb.qf[0:64, :]).then_inc(m, 1))
                if getattr(R.cfg, "debug", False) and pair == 0 and i == 0 and tg == 0:
                    S.wait("sp", R.c_act.sem, t_rot)
                    for col0, srct in ((0, sb.qf), (512, sb.rot),
                                       (1024, sb.cos_t), (1536, sb.sin_t)):
                        tokd = R.ch_dbg.tag(0)
                        S.add("sp", lambda E, c0=col0, s_=srct, m=tokd[0]: E.dma_start(
                            out=R.dr.dbg_p[:, c0:c0 + 512], in_=s_[:, 0:512]).then_inc(m, 16))
                        S.wait_tok("sp", tokd)
                    S.wait("dve", R.ch_dbg.sems[0], R.ch_dbg.cnt[0] * 16)
                S.wait("dve", R.c_act.sem, t_rot)
                S.add("dve", lambda E, tg_=tg: E.tensor_tensor(
                    sb.t1[:, :], sb.qf[:, :],
                    sb.cos_t[:, tg_ * cfg.TH:(tg_ + 1) * cfg.TH], ALU.mult))
                S.add("dve", lambda E, tg_=tg: E.tensor_tensor(
                    sb.t2[:, :], sb.rot[:, :],
                    sb.sin_t[:, tg_ * cfg.TH:(tg_ + 1) * cfg.TH], ALU.mult))
                tq_done = R.c_dve.tag()
                S.add("dve", lambda E, d=QX[i], tg_=tg, m=R.c_dve.sem: E.tensor_tensor(
                    d[:, tg_ * cfg.TH:(tg_ + 1) * cfg.TH], sb.t1[:, :], sb.t2[:, :],
                    ALU.add).then_inc(m, 1))
                S.wait("act", R.c_pe.sem, pe_last[(i, 1)])
                S.wait("act", R.c_dve.sem, tq_done)
                evac_last[i][1] = R.c_act.tag()
                S.add("act", lambda E, p=pk[i], hh_=hh, m=R.c_act.sem: E.activation(
                    sb.qf[:, :], p[:, 0:cfg.TH], AF.Identity,
                    bias=sb.bq[:, 3 * hh_ + 1:3 * hh_ + 2], scale=1.0).then_inc(m, 1))
                S.add("act", lambda E: E.copy(sb.rot[0:64, :], sb.qf[64:128, :]))
                t_rot2 = R.c_act.tag()
                S.add("act", lambda E, m=R.c_act.sem:
                      E.copy(sb.rot[64:128, :], sb.qf[0:64, :]).then_inc(m, 1))
                S.wait("dve", R.c_act.sem, t_rot2)
                S.add("dve", lambda E, tg_=tg: E.tensor_tensor(
                    sb.t1[:, :], sb.qf[:, :],
                    sb.cos_t[:, tg_ * cfg.TH:(tg_ + 1) * cfg.TH], ALU.mult))
                S.add("dve", lambda E, tg_=tg: E.tensor_tensor(
                    sb.t2[:, :], sb.rot[:, :],
                    sb.sin_t[:, tg_ * cfg.TH:(tg_ + 1) * cfg.TH], ALU.mult))
                rope_last[i] = R.c_dve.tag()
                S.add("dve", lambda E, d=KX[i], tg_=tg, m=R.c_dve.sem: E.tensor_tensor(
                    d[:, tg_ * cfg.TH:(tg_ + 1) * cfg.TH], sb.t1[:, :], sb.t2[:, :],
                    ALU.add).then_inc(m, 1))
                S.wait("act", R.c_pe.sem, pe_last[(i, 2)])
                S.wait("act", R.c_dve.sem, vtok_last[i])
                evac_last[i][2] = R.c_act.tag()
                S.add("act", lambda E, p=pv[i], hh_=hh, m=R.c_act.sem: E.activation(
                    sb.vstage[:, :], p[:, 0:cfg.TH], AF.Identity,
                    bias=sb.bq[:, 3 * hh_ + 2:3 * hh_ + 3], scale=1.0).then_inc(m, 1))
                S.wait("pe", R.c_act.sem, evac_last[i][2])
                ntile = cfg.TH // 128
                for jj in range(ntile):
                    gg = R.tr_gg
                    R.tr_gg += 1
                    bank = R.atr[gg % 2]
                    if gg >= 2:
                        S.wait("pe", R.c_dve.sem, R.tr_hist[gg - 2])
                    ttr = R.c_pe.tag()
                    S.add("pe", lambda E, jj_=jj, p=bank, m=R.c_pe.sem: E.transpose(
                        p[:, 0:128], sb.vstage[:, jj_ * 128:(jj_ + 1) * 128],
                        sb.ident[:, :]).then_inc(m, 1))
                    S.wait("dve", R.c_pe.sem, ttr)
                    tev = R.c_dve.tag()
                    S.add("dve", lambda E, d=VX[i], c0=tg * cfg.TH + jj * 128, p=bank,
                          m=R.c_dve.sem: E.tensor_copy(
                        d[:, c0:c0 + 128], p[:, 0:128]).then_inc(m, 1))
                    R.tr_hist.append(tev)
                vtok_last[i] = R.tr_hist[-1]
        R.wq_free[0] = pe_last[(1, 2)]
        R.wq_free[1] = pe_last[(1, 2)]
        if getattr(R.cfg, "debug", False) and pair == 0:
            S.wait("sp", R.c_dve.sem, rope_last[0])
            S.wait("sp", R.c_dve.sem, rope_last[1])
            S.wait("sp", R.c_dve.sem, vtok_last[0])
            for dst, srct in ((R.dr.dbg_q, sb.Qa), (R.dr.dbg_k, sb.Ka), (R.dr.dbg_v, sb.Va)):
                tokd = R.ch_dbg.tag(0)
                S.add("sp", lambda E, d=dst, s=srct, m=tokd[0]:
                      E.dma_start(out=d, in_=s[:, :]).then_inc(m, 16))
                S.wait_tok("sp", tokd)

        for i in range(2):
            hh = (hA, hB)[i]
            deps = [rope_last[0], rope_last[1], vtok_last[0], vtok_last[1], prev_head_done]
            prev_head_done = emit_head_attention(S, R, QX[i], KX[i], VX[i], hh, deps)


def emit_head_attention(S, R, Q, K, V, hh, dve_deps):
    cfg = R.cfg
    sb = R.sb
    ps = [R.aq[0], R.aq[1]]
    po = [R.aq[4], R.aq[5]]
    ptr = R.atr
    seg_id = 0
    seg_evac = {}
    max_tok = [None] * cfg.NQ
    exp_tok = [None] * cfg.NQ
    dve_p = [None] * cfg.NQ
    av_done = [None] * cfg.NQ
    out_evac = [None] * cfg.NQ
    first = True
    for qi in range(cfg.NQ):
        kw = (qi + 1) * 128
        nseg = (kw + 511) // 512
        if getattr(R.cfg, "ser_attn", False):
            S.wait("pe", R.c_dve.sem, R.c_dve.n)
            S.wait("pe", R.c_act.sem, R.c_act.n)
            S.wait("dve", R.c_pe.sem, R.c_pe.n)
            S.wait("act", R.c_pe.sem, R.c_pe.n)
        for s_ in range(nseg):
            w = min(512, kw - s_ * 512)
            if first:
                for d in dve_deps:
                    S.wait("pe", R.c_dve.sem, d)
                first = False
            if seg_id >= 2:
                S.wait("pe", R.c_act.sem, seg_evac[seg_id - 2])
            tokm = R.c_pe.tag()
            S.add("pe", lambda E, p=ps[seg_id % 2], q0=qi * 128, s0=s_ * 512, w_=w,
                  m=R.c_pe.sem: E.matmul(
                p[:, 0:w_], Q[:, q0:q0 + 128], K[:, s0:s0 + w_],
                start=True, stop=True).then_inc(m, 1))
            S.wait("act", R.c_pe.sem, tokm)
            if qi >= 1 and s_ == 0:
                S.wait("act", R.c_dve.sem, max_tok[qi - 1])
            ev = R.c_act.tag()
            seg_evac[seg_id] = ev
            S.add("act", lambda E, p=ps[seg_id % 2], s0=s_ * 512, w_=w, m=R.c_act.sem:
                  E.copy(sb.scores[:, s0:s0 + w_], p[:, 0:w_]).then_inc(m, 1))
            seg_id += 1
        S.wait("dve", R.c_act.sem, seg_evac[seg_id - 1])
        max_tok[qi] = R.c_dve.tag()
        S.add("dve", lambda E, kw_=kw, m=R.c_dve.sem: E.reduce_max(
            sb.negmax[:, :], sb.scores[:, 0:kw_], axis=AX.X, negate=True).then_inc(m, 1))
        S.wait("act", R.c_dve.sem, max_tok[qi])
        if qi >= 1:
            S.wait("act", R.c_pe.sem, av_done[qi - 1])
        exp_tok[qi] = R.c_act.tag()
        S.add("act", lambda E, kw_=kw, m=R.c_act.sem: E.activation(
            sb.P[:, 0:kw_], sb.scores[:, 0:kw_], AF.Exp, bias=sb.negmax[:, 0:1],
            scale=1.0).then_inc(m, 1))
        S.wait("dve", R.c_act.sem, exp_tok[qi])
        S.add("dve", lambda E, q0=qi * 128, kw_=kw: E.tensor_tensor(
            sb.P[:, q0:kw_], sb.P[:, q0:kw_], sb.mask01[:, :], ALU.mult))
        sum_tok = R.c_dve.tag()
        S.add("dve", lambda E, kw_=kw, m=R.c_dve.sem: E.reduce_sum(
            sb.lrow[:, :], sb.P[:, 0:kw_], axis=AX.X).then_inc(m, 1))
        S.wait("act", R.c_dve.sem, sum_tok)
        S.add("act", lambda E: E.activation(sb.rrow[:, :], sb.lrow[:, :], AF.Ln))
        rec_tok = R.c_act.tag()
        S.add("act", lambda E, m=R.c_act.sem: E.activation(
            sb.rrow[:, :], sb.rrow[:, :], AF.Exp, bias=0.0, scale=-1.0).then_inc(m, 1))
        S.wait("dve", R.c_act.sem, rec_tok)
        dve_p[qi] = R.c_dve.tag()
        S.add("dve", lambda E, kw_=kw, m=R.c_dve.sem: E.tensor_scalar(
            sb.P[:, 0:kw_], sb.P[:, 0:kw_], sb.rrow[:, 0:1], None, ALU.mult).then_inc(m, 1))
        S.wait("pe", R.c_dve.sem, dve_p[qi])
        if qi >= 2:
            S.wait("pe", R.c_dve.sem, out_evac[qi - 2])
        av_tok = [None] * (qi + 1)
        pt_tok = [None] * (qi + 1)

        def emit_av(kt):
            S.wait("pe", R.c_dve.sem, pt_tok[kt])
            tok = R.c_pe.tag()
            av_tok[kt] = tok
            if kt == qi:
                av_done[qi] = tok
            S.add("pe", lambda E, p=po[qi % 2], kt_=kt, st=(kt == 0), sp2=(kt == qi),
                  m=R.c_pe.sem: E.matmul(
                p[:, 0:128], V[:, kt_ * 128:(kt_ + 1) * 128],
                sb.PT[kt_ % 2][:, 0:128], start=st, stop=sp2).then_inc(m, 1))

        for kt in range(qi + 1):
            gg = R.tr_gg
            R.tr_gg += 1
            bank = R.atr[gg % 2]
            if gg >= 2:
                S.wait("pe", R.c_dve.sem, R.tr_hist[gg - 2])
            ttr = R.c_pe.tag()
            S.add("pe", lambda E, p=bank, kt_=kt, m=R.c_pe.sem: E.transpose(
                p[:, 0:128], sb.P[:, kt_ * 128:(kt_ + 1) * 128],
                sb.ident[:, :]).then_inc(m, 1))
            if kt >= 1:
                emit_av(kt - 1)
            S.wait("dve", R.c_pe.sem, ttr)
            if kt >= 2:
                S.wait("dve", R.c_pe.sem, av_tok[kt - 2])
            pt_tok[kt] = R.c_dve.tag()
            S.add("dve", lambda E, p=bank, kt_=kt, m=R.c_dve.sem: E.tensor_copy(
                sb.PT[kt_ % 2][:, 0:128], p[:, 0:128]).then_inc(m, 1))
            R.tr_hist.append(pt_tok[kt])
        emit_av(qi)
        S.wait("dve", R.c_pe.sem, av_done[qi])
        if qi == 0 and R.ao_store_toks:
            S.wait_tok("dve", R.ao_store_toks[-1])
        out_evac[qi] = R.c_dve.tag()
        S.add("dve", lambda E, p=po[qi % 2], q0=qi * 128, m=R.c_dve.sem:
              E.tensor_copy(sb.outT[:, q0:q0 + 128], p[:, 0:128]).then_inc(m, 1))
    S.wait("sp", R.c_dve.sem, out_evac[cfg.NQ - 1])
    stok = R.ch_ao.tag(0)
    dst = R.dr.ao_cc[hh * 128:(hh + 1) * 128, :]
    S.add("sp", lambda E, d=dst, m=stok[0]:
          E.dma_start(out=d, in_=sb.outT[:, :]).then_inc(m, 16))
    R.ao_store_toks.append(stok)
    return out_evac[cfg.NQ - 1]


def emit_proj(S, R):
    cfg = R.cfg
    sb = R.sb
    ag_toks = [R.ch_ag.tag(0), R.ch_ag.tag(1)]

    def _gp_ag(E, R=R, cfg=cfg):
        with E.register("agoff") as rg:
            for half in range(2):
                E.reg_load(rg, R.sb.blkoff[0:1, 0:1])
                if half:
                    E.reg_add(rg, rg, cfg.TH)
                ap = bass.AP(R.dr.ao_gath_h, rg,
                             [[cfg.S, 128], [128 * cfg.S, cfg.C], [1, cfg.TH]])
                E.dma_start(out=R.sb.agh[half][:, :, :], in_=ap).then_inc(
                    R.ch_ag.sems[half], 16)
    S.add("gp", _gp_ag)

    pm = [[R.ps0, R.ps1], [R.ps2, R.ps3]]
    pe_m = [[None] * cfg.C for _ in range(2)]
    dve_m = [[None] * cfg.C for _ in range(2)]
    act_e = [[None] * cfg.C for _ in range(2)]
    st_tok = [[None] * cfg.C for _ in range(2)]

    for m_ in range(cfg.C):
        sl = m_ % 2
        if m_ >= 2:
            S.wait("gp", R.c_pe.sem, pe_m[1][m_ - 2])
        twp = R.ch_wp.tag(sl)
        src = R.dr.wproj[:, m_ * 128:(m_ + 1) * 128].rearrange("(c p) i -> p c i", p=128)
        S.add("gp", lambda E, d=sb.wp2[sl], s=src, m=twp[0]:
              E.dma_start(out=d[:, :, :], in_=s).then_inc(m, 16))
        for half in range(2):
            ch_res = R.ch_x1r if half == 0 else R.ch_x
            ch_st = R.ch_sx2 if half == 0 else R.ch_sx1
            if m_ >= 2:
                S.wait("sp", R.c_dve.sem, dve_m[half][m_ - 2])
            tx = ch_res.tag(sl)
            src_x = R.dr.x1_save[m_ * 128:(m_ + 1) * 128, half * cfg.TH:(half + 1) * cfg.TH]
            S.add("sp", lambda E, d=sb.xin2[half][sl], s=src_x, m=tx[0]:
                  E.dma_start(out=d[:, :], in_=s).then_inc(m, 16))
            if half == 0:
                S.wait_tok("pe", twp)
                if m_ == 0:
                    S.wait_tok("pe", ag_toks[0])
                    S.wait_tok("pe", ag_toks[1])
            if m_ >= 2:
                S.wait("pe", R.c_act.sem, act_e[half][m_ - 2])
            for c in range(cfg.C):
                if c == cfg.C - 1:
                    pe_m[half][m_] = R.c_pe.tag()
                    S.add("pe", lambda E, p=pm[half][sl], h_=half, c_=c, w=sb.wp2[sl],
                          m=R.c_pe.sem: E.matmul(
                        p[:, :], w[:, c_, :], sb.agh[h_][:, c_, :],
                        start=False, stop=True).then_inc(m, 1))
                else:
                    S.add("pe", lambda E, p=pm[half][sl], h_=half, c_=c, w=sb.wp2[sl],
                          st=(c == 0): E.matmul(
                        p[:, :], w[:, c_, :], sb.agh[h_][:, c_, :], start=st, stop=False))
            S.wait("act", R.c_pe.sem, pe_m[half][m_])
            if m_ >= 2:
                S.wait("act", R.c_dve.sem, dve_m[half][m_ - 2])
            act_e[half][m_] = R.c_act.tag()
            S.add("act", lambda E, p=pm[half][sl], h_=half, m2=m_, sl_=sl, m=R.c_act.sem:
                  E.activation(sb.tmpd2[h_][sl_][:, :], p[:, :], AF.Identity,
                               bias=sb.bproj[:, m2:m2 + 1], scale=1.0).then_inc(m, 1))
            S.wait("dve", R.c_act.sem, act_e[half][m_])
            S.wait_tok("dve", tx)
            if m_ >= 2:
                S.wait_tok("dve", st_tok[half][m_ - 2])
            dve_m[half][m_] = R.c_dve.tag()
            S.add("dve", lambda E, h_=half, sl_=sl, m=R.c_dve.sem: E.tensor_tensor(
                sb.x2stage[h_][sl_][:, :], sb.tmpd2[h_][sl_][:, :],
                sb.xin2[h_][sl_][:, :], ALU.add).then_inc(m, 1))
            S.wait("sp", R.c_dve.sem, dve_m[half][m_])
            stok = ch_st.tag(sl)
            st_tok[half][m_] = stok
            dst = R.dr.x2_save[m_ * 128:(m_ + 1) * 128, half * cfg.TH:(half + 1) * cfg.TH]
            S.add("sp", lambda E, d=dst, h_=half, sl_=sl, m=stok[0]:
                  E.dma_start(out=d, in_=sb.x2stage[h_][sl_][:, :]).then_inc(m, 16))
    prev = [(R.c_dve.sem, dve_m[1][cfg.C - 2]), (R.c_dve.sem, dve_m[1][cfg.C - 1])]
    sqp = ()
    for half in range(2):
        R.sb.xin = sb.xin2[half]
        R.sb.sq = sb.sq2[half]
        pass_prev = list(prev) + [(R.c_dve.sem, dve_m[half][cfg.C - 2]),
                                  (R.c_dve.sem, dve_m[half][cfg.C - 1])]
        sq_pass = list(sqp) + [(R.c_dve.sem, dve_m[half][cfg.C - 2]),
                               (R.c_dve.sem, dve_m[half][cfg.C - 1]),
                               (R.c_act.sem, act_e[half][cfg.C - 1])]
        stats, prev, sqp = emit_ln_stats_stream(
            S, R, R.dr.x2_save, half, store_toks=st_tok[half],
            prev_waits=pass_prev, sq_prev=sq_pass,
            psums=(R.pstat[2 * half], R.pstat[2 * half + 1]))
        R.ab3_ready[half] = emit_ln_rows_and_bcast(
            S, R, R.pstat[2 * half], R.pstat[2 * half + 1], stats,
            (None, None), scr_rows=2 + 2 * half, defer=True)



def build_nc(cfg):
    nc = bass.Bass("TRN2", target_bir_lowering=False, detect_race_conditions=False,
                   dynamic_dma_scratch_size=16384)
    D, I, S_, TPC, TH, C, IC = cfg.D, cfg.I, cfg.S, cfg.TPC, cfg.TH, cfg.C, cfg.IC

    dr = SimpleNamespace()

    def din(name, shape, dt):
        return nc.dram_tensor(name, shape, dt, kind="ExternalInput").ap()

    dr.x_fm = din("x_fm", [D, TPC], F32)
    dr.wg_hb = din("wg_hb", [IC, 128, C, 128], I32)
    dr.wu_hb = din("wu_hb", [IC, 128, C, 128], I32)
    dr.wd_hb = din("wd_hb", [C, 128, IC, 128], I32)
    dr.wqkv = din("wqkv", [D, cfg.HPC * 384], F32)
    dr.bq_d = din("bqkv", [128, 3 * cfg.HPC], F32)
    dr.wproj = din("wproj", [D, D], F32)
    dr.bproj_d = din("bproj2", [128, C], F32)
    dr.scl = din("scl", [1, 3], F32)
    dr.cvals = din("cvals", [1, 2], F32)
    dr.cos_t = din("cos_t", [128, S_], F32)
    dr.sin_t = din("sin_t", [128, S_], F32)
    dr.ident = din("ident", [128, 128], BF16)
    dr.mask01 = din("mask01", [128, 128], BF16)
    dr.blkoff = din("blkoff", [1, 1], I32)
    dbg_kind = "ExternalOutput" if getattr(cfg, "debug", False) else "Internal"
    dr.out_fm = nc.dram_tensor("out_fm", [D, TPC], F32, kind="ExternalOutput").ap()
    dr.x1_save = nc.dram_tensor("x1_save", [D, TPC], F32, kind=dbg_kind).ap()
    dr.bc_scr_t = nc.dram_tensor("bc_scr", [6, TH], F32)
    dr.bc_scr = dr.bc_scr_t.ap()
    dr.x2_save = nc.dram_tensor("x2_save", [D, TPC], F32, kind=dbg_kind).ap()
    dr.y2_cc_h = nc.dram_tensor("y2_cc", [D, TPC], BF16)
    dr.y2_cc_ap = dr.y2_cc_h.ap()
    dr.y2_gath_h = nc.dram_tensor("y2_gath", [4 * D, TPC], BF16)
    dr.y2_gath = dr.y2_gath_h.ap()
    dr.ao_cc_h = nc.dram_tensor("ao_cc", [cfg.HPC * 128, S_], BF16)
    dr.ao_cc = dr.ao_cc_h.ap()
    dr.ao_gath_h = nc.dram_tensor("ao_gath", [4 * cfg.HPC * 128, S_], BF16)
    if getattr(cfg, "debug", False):
        dr.dbg_y2cc = nc.dram_tensor("y2_cc_dbg", [4 * D, TPC], BF16, kind="ExternalOutput").ap()
        dr.dbg_aocc = nc.dram_tensor("ao_cc_dbg", [4 * cfg.HPC * 128, S_], BF16, kind="ExternalOutput").ap()
        dr.dbg_aoctr = nc.dram_tensor("ao_ctr_dbg", [cfg.HPC * 128, S_], BF16, kind="ExternalOutput").ap()
        dr.dbg_y = nc.dram_tensor("dbg_y", [128, C, TH], BF16).ap()
        dr.dbg_rows = nc.dram_tensor("dbg_rows", [1, 2 * TH], F32, kind="ExternalOutput").ap()
        dr.dbg_rows2 = nc.dram_tensor("dbg_rows2", [1, 2 * TH], F32, kind="ExternalOutput").ap()
        dr.dbg_ab = nc.dram_tensor("dbg_ab", [128, 2 * TH], F32, kind="ExternalOutput").ap()
        dr.dbg_h = nc.dram_tensor("dbg_h", [128, IC, TH], BF16).ap()
        dr.dbg_q = nc.dram_tensor("dbg_q", [128, S_], BF16, kind="ExternalOutput").ap()
        dr.dbg_k = nc.dram_tensor("dbg_k", [128, S_], BF16, kind="ExternalOutput").ap()
        dr.dbg_v = nc.dram_tensor("dbg_v", [128, S_], BF16, kind="ExternalOutput").ap()
        dr.dbg_p = nc.dram_tensor("dbg_p", [128, S_], F32, kind="ExternalOutput").ap()

    with ExitStack() as ctx:
        R = SimpleNamespace()
        R.cfg, R.dr = cfg, dr
        R.c_pe = Ctr(nc, ctx, "cpe")
        R.c_dve = Ctr(nc, ctx, "cdve")
        R.c_act = Ctr(nc, ctx, "cact")
        R.c_cc = Ctr(nc, ctx, "ccc")
        R.ch_const = Chan(nc, ctx, "chc", 1)
        R.ch_x = Chan(nc, ctx, "chx", 2)
        R.ch_x1r = Chan(nc, ctx, "chxr", 2)
        R.ch_wg = Chan(nc, ctx, "chwg", 2)
        R.ch_wu = Chan(nc, ctx, "chwu", 2)
        R.ch_wd = Chan(nc, ctx, "chwd", 2)
        R.ch_sx1 = Chan(nc, ctx, "chs1", 2)
        R.ch_sx2 = Chan(nc, ctx, "chs2", 2)
        R.ch_y2w = Chan(nc, ctx, "chyw", 1)
        R.ch_y2r = Chan(nc, ctx, "chyr", 2)
        R.ch_wq = Chan(nc, ctx, "chwq", 2)
        R.ch_ao = Chan(nc, ctx, "chao", 1)
        R.ch_ag = Chan(nc, ctx, "chag", 2)
        R.ch_wp = Chan(nc, ctx, "chwp", 2)
        R.ch_dbg = Chan(nc, ctx, "chdbg", 1)
        R.ch_cs = Chan(nc, ctx, "chcs", 1)
        R.ch_bcw = Chan(nc, ctx, "chbw", 2)
        R.ch_bcr = Chan(nc, ctx, "chbr", 2)

        sb = SimpleNamespace()
        R.sb = sb
        sb.ones_col = ctx.enter_context(nc.sbuf_tensor("ones_col", [128, 1], F32))
        sb.ones_row = ctx.enter_context(nc.sbuf_tensor("ones_row", [1, 128], F32))
        sb.scl_row = ctx.enter_context(nc.sbuf_tensor("scl_row", [1, 3], F32))
        sb.cvals = ctx.enter_context(nc.sbuf_tensor("cvals_sb", [1, 2], F32))
        nc.const_aps.register(F32, EPS, sb.cvals[0:1, 0:1])
        nc.const_aps.register(F32, QSC, sb.cvals[0:1, 1:2])
        sb.scl_b = ctx.enter_context(nc.sbuf_tensor("scl_b", [128, 4], F32))
        sb.ident = ctx.enter_context(nc.sbuf_tensor("identt", [128, 128], BF16))
        sb.mask01 = ctx.enter_context(nc.sbuf_tensor("mask01t", [128, 128], BF16))
        sb.bq = ctx.enter_context(nc.sbuf_tensor("bqt", [128, 3 * cfg.HPC], F32))
        sb.bqs = ctx.enter_context(nc.sbuf_tensor("bqst", [128, 2], F32))
        sb.bproj = ctx.enter_context(nc.sbuf_tensor("bprojt", [128, C], F32))
        sb.blkoff = ctx.enter_context(nc.sbuf_tensor("blkofft", [1, 1], I32))
        sb.rows = [ctx.enter_context(nc.sbuf_tensor(f"rows{i}", [1, TH], F32)) for i in range(4)]
        if getattr(cfg, "debug", False):
            sb.dbg_rows_sb = ctx.enter_context(nc.sbuf_tensor("dbg_rows_sb", [1, TH], F32))

        R.y2_store_toks = []
        R.bc_prev = None
        R.ab_free = None
        R.ao_store_toks = []
        R.ab3_ready = [None, None]
        R.wq_free = [None, None]
        R.y2_cons = {}
        R.tr_gg = 0
        R.tr_hist = []

        with nc.psum_tensor("pinit", [128, 8], F32) as ps0:
            R.ps0 = ps0
            S = Sched()
            emit_init(S, R)
            run_block(nc, S)

        def ffn_scope(fctx, tag):
            sb.y = fctx.enter_context(nc.sbuf_tensor(f"y{tag}", [128, C, TH], BF16))
            sb.h = fctx.enter_context(nc.sbuf_tensor(f"h{tag}", [128, IC, TH], BF16))
            sb.xin = [fctx.enter_context(nc.sbuf_tensor(f"xin{tag}{i}", [128, TH], F32)) for i in range(2)]
            sb.tmpd = [fctx.enter_context(nc.sbuf_tensor(f"tmpd{tag}{i}", [128, TH], F32)) for i in range(2)]
            sb.tmp = sb.tmpd[0]
            sb.x1stage = [fctx.enter_context(nc.sbuf_tensor(f"x1s{tag}{i}", [128, TH], F32)) for i in range(2)]
            sb.ustage = [fctx.enter_context(nc.sbuf_tensor(f"ust{tag}{i}", [128, TH], BF16)) for i in range(2)]
            sb.wg2 = [fctx.enter_context(nc.sbuf_tensor(f"wg2{tag}{i}", [128, C, 128], BF16)) for i in range(2)]
            sb.wu2 = [fctx.enter_context(nc.sbuf_tensor(f"wu2{tag}{i}", [128, C, 128], BF16)) for i in range(2)]
            sb.wd2 = [fctx.enter_context(nc.sbuf_tensor(f"wd2{tag}{i}", [128, cfg.ICH, 128], BF16)) for i in range(2)]
            sb.sq = sb.tmpd
            sb.A = sb.x1stage[0]
            sb.B = sb.x1stage[1]

        with ExitStack() as fctx:
            ffn_scope(fctx, "a")
            with (nc.psum_tensor("pf0", [128, TH], F32) as ps0,
                  nc.psum_tensor("pf1", [128, TH], F32) as ps1,
                  nc.psum_tensor("pf2", [128, TH], F32) as ps2,
                  nc.psum_tensor("pf3", [128, TH], F32) as ps3):
                R.ps0, R.ps1, R.ps2, R.ps3 = ps0, ps1, ps2, ps3
                for half in range(2):
                    S = Sched()
                    stats, xin_free, sq_free = emit_ln_stats_stream(S, R, dr.x_fm, half)
                    ab_ready = emit_ln_rows_and_bcast(S, R, R.ps2, R.ps3, stats, (sb.A, sb.B))
                    y_ready, norm_free = emit_normalize_stream(
                        S, R, dr.x_fm, half, (sb.A, sb.B), ab_ready, prev_waits=xin_free)
                    if getattr(cfg, "debug", False) and half == 0:
                        emit_debug_dump(S, R, sb.y[:, :, :], dr.dbg_y, R.c_dve, y_ready)
                    emit_ffn(S, R, app=1, half=half, y_ready=y_ready,
                             xin_prev=norm_free, sq_prev=sq_free)
                    if getattr(cfg, "debug", False) and half == 0:
                        emit_debug_dump(S, R, sb.h[:, :, :], dr.dbg_h, R.c_dve, R.c_dve.n)
                    run_block(nc, S)

        S = Sched()
        emit_collective(S, R, 1)
        run_block(nc, S)

        with ExitStack() as actx:
            sb.cos_t = actx.enter_context(nc.sbuf_tensor("cos_tt", [128, S_], F32))
            sb.sin_t = actx.enter_context(nc.sbuf_tensor("sin_tt", [128, S_], F32))
            sb.wq = actx.enter_context(nc.sbuf_tensor("wq", [128, C, 768], BF16))
            sb.Qa = actx.enter_context(nc.sbuf_tensor("Qa", [128, S_], BF16))
            sb.Qb = actx.enter_context(nc.sbuf_tensor("Qb", [128, S_], BF16))
            sb.Ka = actx.enter_context(nc.sbuf_tensor("Ka", [128, S_], BF16))
            sb.Kb = actx.enter_context(nc.sbuf_tensor("Kb", [128, S_], BF16))
            sb.Va = actx.enter_context(nc.sbuf_tensor("Vaa", [128, S_], BF16))
            sb.Vb = actx.enter_context(nc.sbuf_tensor("Vbb", [128, S_], BF16))
            sb.y2sb = [actx.enter_context(nc.sbuf_tensor(f"y2sb{i}", [128, cfg.CG, TH], BF16))
                       for i in range(2)]
            sb.qf = actx.enter_context(nc.sbuf_tensor("qf", [128, TH], F32))
            sb.rot = actx.enter_context(nc.sbuf_tensor("rotb", [128, TH], F32))
            sb.t1 = actx.enter_context(nc.sbuf_tensor("t1b", [128, TH], F32))
            sb.t2 = actx.enter_context(nc.sbuf_tensor("t2b", [128, TH], F32))
            sb.vstage = actx.enter_context(nc.sbuf_tensor("vstage", [128, TH], BF16))
            sb.scores = actx.enter_context(nc.sbuf_tensor("scores", [128, S_], F32))
            sb.P = actx.enter_context(nc.sbuf_tensor("Pm", [128, S_], BF16))
            sb.PT = [actx.enter_context(nc.sbuf_tensor(f"PTs{i}", [128, 512], BF16)) for i in range(2)]
            sb.outT = actx.enter_context(nc.sbuf_tensor("outTt", [128, S_], BF16))
            sb.negmax = actx.enter_context(nc.sbuf_tensor("negmax", [128, 1], F32))
            sb.lrow = actx.enter_context(nc.sbuf_tensor("lrowt", [128, 1], F32))
            sb.rrow = actx.enter_context(nc.sbuf_tensor("rrowt", [128, 1], F32))
            with (nc.psum_tensor("pa0", [128, 512], F32) as a0,
                  nc.psum_tensor("pa1", [128, 512], F32) as a1,
                  nc.psum_tensor("pa2", [128, 512], F32) as a2,
                  nc.psum_tensor("pa3", [128, 512], F32) as a3,
                  nc.psum_tensor("pa4", [128, 512], F32) as a4,
                  nc.psum_tensor("pa5", [128, 512], F32) as a5,
                  nc.psum_tensor("pat0", [128, 1024], BF16) as at0,
                  nc.psum_tensor("pat1", [128, 1024], BF16) as at1):
                R.aq = [a0, a1, a2, a3, a4, a5]
                R.atr = [at0, at1]
                S = Sched()
                emit_attention(S, R)
                run_block(nc, S)

        S = Sched()
        emit_collective(S, R, 2)
        run_block(nc, S)

        with ExitStack() as pctx:
            sb.agh = [pctx.enter_context(nc.sbuf_tensor(f"agh{i}", [128, C, TH], BF16))
                      for i in range(2)]
            sb.wp2 = [pctx.enter_context(nc.sbuf_tensor(f"wp2_{i}", [128, C, 128], BF16))
                      for i in range(2)]
            sb.xin2 = [[pctx.enter_context(nc.sbuf_tensor(f"xin2_{h}_{i}", [128, TH], F32))
                        for i in range(2)] for h in range(2)]
            sb.tmpd2 = [[pctx.enter_context(nc.sbuf_tensor(f"tmpd2_{h}_{i}", [128, TH], F32))
                         for i in range(2)] for h in range(2)]
            sb.x2stage = [[pctx.enter_context(nc.sbuf_tensor(f"x2s_{h}_{i}", [128, TH], F32))
                           for i in range(2)] for h in range(2)]
            sb.sq2 = sb.tmpd2
            with (nc.psum_tensor("pp0", [128, TH], F32) as ps0,
                  nc.psum_tensor("pp1", [128, TH], F32) as ps1,
                  nc.psum_tensor("pp2", [128, TH], F32) as ps2,
                  nc.psum_tensor("pp3", [128, TH], F32) as ps3,
                  nc.psum_tensor("pq0", [128, TH], F32) as pt0,
                  nc.psum_tensor("pq1", [128, TH], F32) as pt1,
                  nc.psum_tensor("pq2", [128, TH], F32) as pt2,
                  nc.psum_tensor("pq3", [128, TH], F32) as pt3):
                R.ps0, R.ps1, R.ps2, R.ps3 = ps0, ps1, ps2, ps3
                R.pstat = [pt0, pt1, pt2, pt3]
                S = Sched()
                emit_proj(S, R)
                run_block(nc, S)

        with ExitStack() as fctx:
            ffn_scope(fctx, "b")
            with (nc.psum_tensor("pg0", [128, TH], F32) as ps0,
                  nc.psum_tensor("pg1", [128, TH], F32) as ps1,
                  nc.psum_tensor("pg2", [128, TH], F32) as ps2,
                  nc.psum_tensor("pg3", [128, TH], F32) as ps3):
                R.ps0, R.ps1, R.ps2, R.ps3 = ps0, ps1, ps2, ps3
                for half in range(2):
                    S = Sched()
                    for t in R.ab3_ready[half]:
                        S.wait_tok("sp", t)
                    ab_ready = emit_ab_load(S, R, 2 + 2 * half, sb.A, sb.B)
                    y_ready, norm_free = emit_normalize_stream(
                        S, R, dr.x2_save, half, (sb.A, sb.B), ab_ready)
                    emit_ffn(S, R, app=2, half=half, y_ready=y_ready,
                             xin_prev=norm_free)
                    run_block(nc, S)

    return nc



def host_prepare(cfg, x, w_attn, b_attn, w_proj, b_proj, gate_w, gate_s, up_w, up_s,
                 down_w, down_s):
    D, I, S_, TPC, C, IC, HPC = cfg.D, cfg.I, cfg.S, cfg.TPC, cfg.C, cfg.IC, cfg.HPC
    x = np.asarray(x, dtype=np.float32)
    w_attn = np.asarray(w_attn, dtype=np.float32)
    b_attn = np.asarray(b_attn, dtype=np.float32)
    w_proj = np.ascontiguousarray(np.asarray(w_proj, dtype=np.float32))
    b_proj = np.asarray(b_proj, dtype=np.float32)
    gate_w = np.asarray(gate_w, dtype=np.int32)
    up_w = np.asarray(up_w, dtype=np.int32)
    down_w = np.asarray(down_w, dtype=np.int32)

    wg_hb = np.ascontiguousarray(gate_w.reshape(IC, 128, C, 128).transpose(0, 3, 2, 1))
    wu_hb = np.ascontiguousarray(up_w.reshape(IC, 128, C, 128).transpose(0, 3, 2, 1))
    wd_hb = np.ascontiguousarray(down_w.reshape(C, 128, IC, 128).transpose(0, 3, 2, 1))
    scl = np.array([[float(np.asarray(gate_s).reshape(-1)[0]),
                     float(np.asarray(up_s).reshape(-1)[0]),
                     float(np.asarray(down_s).reshape(-1)[0])]], dtype=np.float32)
    bproj2 = np.ascontiguousarray(b_proj.reshape(C, 128).T)

    inv = 1.0 / (10000.0 ** (np.arange(0, 128, 2, dtype=np.float64) / 128.0))
    pos = np.arange(S_, dtype=np.float64)
    ang = inv[:, None] * pos[None, :]
    cos_t = np.ascontiguousarray(np.concatenate([np.cos(ang), np.cos(ang)], 0).astype(np.float32))
    sin_t = np.ascontiguousarray(np.concatenate([-np.sin(ang), np.sin(ang)], 0).astype(np.float32))
    ident = np.eye(128, dtype=np.float32).astype(ml_dtypes.bfloat16)
    mask01 = np.tril(np.ones((128, 128), dtype=np.float32)).astype(ml_dtypes.bfloat16)

    in_maps = []
    for r in range(8):
        s, b = r // 4, r % 4
        x_fm = np.ascontiguousarray(x[s, b * TPC:(b + 1) * TPC, :].T)
        h0 = HPC * b
        cols, bcols = [], []
        for hh in range(HPC):
            h = h0 + hh
            for part in range(3):
                cols.append(w_attn[:, part * D + h * 128:part * D + (h + 1) * 128])
                bcols.append(b_attn[part * D + h * 128:part * D + (h + 1) * 128])
        wqkv = np.ascontiguousarray(np.concatenate(cols, axis=1))
        bqkv = np.ascontiguousarray(np.stack(bcols, axis=1))
        in_maps.append(dict(
            x_fm=x_fm, wg_hb=wg_hb, wu_hb=wu_hb, wd_hb=wd_hb, wqkv=wqkv, bqkv=bqkv,
            wproj=w_proj, bproj2=bproj2, scl=scl, cos_t=cos_t, sin_t=sin_t,
            ident=ident, mask01=mask01,
            cvals=np.array([[EPS, QSC]], dtype=np.float32),
            blkoff=np.array([[b * TPC]], dtype=np.int32),
        ))
    return in_maps


def host_assemble(cfg, results):
    out = np.empty((2, cfg.S, cfg.D), dtype=np.float32)
    for r in range(8):
        s, b = r // 4, r % 4
        out[s, b * cfg.TPC:(b + 1) * cfg.TPC, :] = np.asarray(results[r]["out_fm"]).T
    return out


def kernel(L0_attn_hidden_states, L0_attn_w_attn, L0_attn_b_attn, L0_attn_w_proj,
           L0_attn_b_proj, L0_ffn_gate_w, L0_ffn_gate_s, L0_ffn_up_w, L0_ffn_up_s,
           L0_ffn_down_w, L0_ffn_down_s):
    cfg = FULL
    in_maps = host_prepare(cfg, L0_attn_hidden_states, L0_attn_w_attn, L0_attn_b_attn,
                           L0_attn_w_proj, L0_attn_b_proj, L0_ffn_gate_w, L0_ffn_gate_s,
                           L0_ffn_up_w, L0_ffn_up_s, L0_ffn_down_w, L0_ffn_down_s)
    nc = build_nc(cfg)
    res = run_bass_kernel_spmd(nc, in_maps, core_ids=list(range(8)))
    return host_assemble(cfg, res.results)


# version 03 (final)
# speedup vs baseline: 1.0306x; 1.0306x over previous
import math
import numpy as np
import ml_dtypes
from contextlib import ExitStack
from types import SimpleNamespace

import concourse.bass as bass
import concourse.mybir as mybir
from concourse.bass_utils import run_bass_kernel_spmd

F32 = mybir.dt.float32
BF16 = mybir.dt.bfloat16
I32 = mybir.dt.int32
AF = mybir.ActivationFunctionType
ALU = mybir.AluOpType
AX = mybir.AxisListType

EPS = 1e-5
QSC = 1.0 / math.sqrt(128.0)


def make_cfg(D, I, S):
    c = SimpleNamespace()
    c.D, c.I, c.S = D, I, S
    c.C = D // 128
    c.IC = I // 128
    c.ICH = c.IC // 2
    c.TPC = S // 4
    c.TH = c.TPC // 2
    c.H = D // 128
    c.HPC = c.H // 4
    c.PAIRS = c.HPC // 2
    c.NQ = S // 128
    c.TGN = S // c.TH
    c.CG = min(8, c.C)
    c.CGN = c.C // c.CG
    return c


FULL = make_cfg(4096, 11008, 4096)


class Ctr:

    def __init__(self, nc, ctx, name):
        self.sem = ctx.enter_context(nc.semaphore(name))
        self.n = 0

    def tag(self):
        self.n += 1
        return self.n


class Chan:

    def __init__(self, nc, ctx, name, nslots):
        self.sems = [ctx.enter_context(nc.semaphore(f"{name}{i}")) for i in range(nslots)]
        self.cnt = [0] * nslots

    def tag(self, slot):
        self.cnt[slot] += 1
        return (self.sems[slot], 16 * self.cnt[slot])

    def total(self, slot=0):
        return (self.sems[slot], 16 * self.cnt[slot])


class Sched:
    ENGINES = ("pe", "dve", "act", "sp", "gp")

    def __init__(self):
        self.q = {e: [] for e in self.ENGINES}

    def add(self, eng, fn):
        self.q[eng].append(fn)

    def wait(self, eng, sem, val):
        if val is None:
            return
        self.q[eng].append(lambda E, s=sem, v=val: E.wait_ge(s, v))

    def wait_tok(self, eng, tok):
        if tok is None:
            return
        sem, val = tok
        self.wait(eng, sem, val)


def run_block(nc, S):
    with nc.Block() as blk:
        if S.q["pe"]:
            @blk.tensor
            def _(E):
                for fn in S.q["pe"]:
                    fn(E)
        if S.q["dve"]:
            @blk.vector
            def _(E):
                for fn in S.q["dve"]:
                    fn(E)
        if S.q["act"]:
            @blk.scalar
            def _(E):
                for fn in S.q["act"]:
                    fn(E)
        if S.q["sp"]:
            @blk.sync
            def _(E):
                for fn in S.q["sp"]:
                    fn(E)
        if S.q["gp"]:
            @blk.gpsimd
            def _(E):
                for fn in S.q["gp"]:
                    fn(E)



def emit_init(S, R):
    sb, dr = R.sb, R.dr
    ch = R.ch_const
    pairs = [
        (sb.ident, dr.ident), (sb.mask01, dr.mask01),
        (sb.bq, dr.bq_d), (sb.bproj, dr.bproj_d),
        (sb.scl_row, dr.scl), (sb.blkoff, dr.blkoff),
        (sb.cvals, dr.cvals),
    ]
    for dst, src in pairs:
        ch.tag(0)
        S.add("gp", lambda E, d=dst, s=src, m=ch.sems[0]:
              E.dma_start(out=d[:, :], in_=s[:, :]).then_inc(m, 16))
    total = ch.total(0)

    S.add("dve", lambda E: E.memset(R.sb.ones_col[:, :], 1.0))
    t_ones = R.c_dve.tag()
    S.add("dve", lambda E, m=R.c_dve.sem:
          E.memset(R.sb.ones_row[:, :], 1.0).then_inc(m, 1))

    S.wait_tok("pe", total)
    S.wait("pe", R.c_dve.sem, t_ones)
    t_bc = R.c_pe.tag()
    S.add("pe", lambda E, m=R.c_pe.sem: E.matmul(
        R.ps0[:, 0:3], R.sb.ones_row[:, :], R.sb.scl_row[0:1, :],
        start=True, stop=True).then_inc(m, 1))
    S.wait("act", R.c_pe.sem, t_bc)
    t_ev = R.c_act.tag()
    S.add("act", lambda E, m=R.c_act.sem:
          E.copy(R.sb.scl_b[:, 0:3], R.ps0[:, 0:3]).then_inc(m, 1))
    S.wait("dve", R.c_act.sem, t_ev)
    t_ds = R.c_dve.tag()
    S.add("dve", lambda E, m=R.c_dve.sem: E.tensor_scalar(
        R.sb.scl_b[:, 3:4], R.sb.scl_b[:, 2:3], 0.5, None, ALU.mult).then_inc(m, 1))
    S.wait_tok("gp", total)
    S.wait("gp", R.c_dve.sem, t_ds)


def emit_ln_rows_and_bcast(S, R, p_sum, p_sq, stats_toks, dst_ab, scr_rows=0, defer=False):
    cfg = R.cfg
    A, B = dst_ab
    r = R.sb.rows
    S.wait("dve", R.c_pe.sem, stats_toks[0])
    S.wait("dve", R.c_pe.sem, stats_toks[1])
    S.add("dve", lambda E, p=p_sum: E.tensor_scalar(
        r[0][0:1, :], p[0:1, :], 1.0 / cfg.D, None, ALU.mult))
    S.add("dve", lambda E, p=p_sq: E.tensor_scalar(
        r[1][0:1, :], p[0:1, :], 1.0 / cfg.D, None, ALU.mult))
    S.add("dve", lambda E: E.tensor_tensor(r[3][0:1, :], r[0][0:1, :], r[0][0:1, :], ALU.mult))
    t_var = R.c_dve.tag()
    S.add("dve", lambda E, m=R.c_dve.sem: E.tensor_tensor(
        r[2][0:1, :], r[1][0:1, :], r[3][0:1, :], ALU.subtract).then_inc(m, 1))
    S.wait("act", R.c_dve.sem, t_var)
    S.add("act", lambda E: E.activation(
        r[1][0:1, :], r[2][0:1, :], AF.Ln, bias=EPS, scale=1.0))
    t_sqrt = R.c_act.tag()
    S.add("act", lambda E, m=R.c_act.sem: E.activation(
        r[2][0:1, :], r[1][0:1, :], AF.Exp, bias=0.0, scale=-0.5).then_inc(m, 1))
    S.wait("dve", R.c_act.sem, t_sqrt)
    S.add("dve", lambda E: E.tensor_tensor(r[3][0:1, :], r[0][0:1, :], r[2][0:1, :], ALU.mult))
    t_ab = R.c_dve.tag()
    S.add("dve", lambda E, m=R.c_dve.sem: E.tensor_scalar(
        r[3][0:1, :], r[3][0:1, :], -1.0, None, ALU.mult).then_inc(m, 1))
    TH = r[0].shape[1]
    S.wait("sp", R.c_dve.sem, t_ab)
    S.wait("sp", R.c_act.sem, t_sqrt)
    if R.bc_prev is not None:
        for sem_, val_ in R.bc_prev:
            S.wait("sp", sem_, val_)
    row0 = scr_rows
    w1 = R.ch_bcw.tag(0)
    S.add("sp", lambda E, r0=row0, m=w1[0]: E.dma_start(
        out=R.dr.bc_scr[r0:r0 + 1, :], in_=r[2][0:1, :]).then_inc(m, 16))
    w2 = R.ch_bcw.tag(1)
    S.add("sp", lambda E, r0=row0, m=w2[0]: E.dma_start(
        out=R.dr.bc_scr[r0 + 1:r0 + 2, :], in_=r[3][0:1, :]).then_inc(m, 16))
    S.wait_tok("sp", w1)
    S.wait_tok("sp", w2)
    if defer:
        return (w1, w2)
    return emit_ab_load(S, R, row0, A, B)


def emit_ab_load(S, R, row0, A, B):
    TH = A.shape[1]
    if R.ab_free is not None:
        for sem_, val_ in R.ab_free:
            S.wait("sp", sem_, val_)
    rA = R.ch_bcr.tag(0)
    S.add("sp", lambda E, d=A, r0=row0, m=rA[0]: E.dma_start(
        out=d[:, :], in_=bass.AP(R.dr.bc_scr_t, r0 * TH, [[0, 128], [1, TH]])).then_inc(m, 16))
    rB = R.ch_bcr.tag(1)
    S.add("sp", lambda E, d=B, r0=row0, m=rB[0]: E.dma_start(
        out=d[:, :], in_=bass.AP(R.dr.bc_scr_t, (r0 + 1) * TH, [[0, 128], [1, TH]])).then_inc(m, 16))
    R.bc_prev = [rA, rB]
    return (rA, rB)


def emit_ln_stats_stream(S, R, src_dram, half, bank_guard=(None, None),
                         store_toks=None, prev_waits=(), sq_prev=(), psums=None):
    cfg = R.cfg
    p_sum, p_sq = psums if psums is not None else (R.ps2, R.ps3)
    xin, sq = R.sb.xin, R.sb.sq
    pe_x = [None] * cfg.C
    act_x = [None] * cfg.C
    pe_q = [None] * cfg.C
    for c in range(cfg.C):
        sl = c % 2
        if c == 0:
            for sem_, val_ in prev_waits:
                S.wait("sp", sem_, val_)
        if store_toks is not None:
            S.wait_tok("sp", store_toks[c])
        if c >= 2:
            S.wait("sp", R.c_pe.sem, pe_x[c - 2])
            S.wait("sp", R.c_act.sem, act_x[c - 2])
        tok = R.ch_x.tag(sl)
        src_ap = src_dram[c * 128:(c + 1) * 128, half * cfg.TH:(half + 1) * cfg.TH]
        S.add("sp", lambda E, d=xin[sl], s=src_ap, m=tok[0]:
              E.dma_start(out=d[:, :], in_=s).then_inc(m, 16))
        S.wait_tok("act", tok)
        if c < 2:
            for sem_, val_ in sq_prev:
                S.wait("act", sem_, val_)
        if c >= 2:
            S.wait("act", R.c_pe.sem, pe_q[c - 2])
        act_x[c] = R.c_act.tag()
        S.add("act", lambda E, d=sq[sl], s=xin[sl], m=R.c_act.sem:
              E.activation(d[:, :], s[:, :], AF.Square).then_inc(m, 1))
        S.wait_tok("pe", tok)
        if c == 0:
            S.wait("pe", R.c_act.sem, bank_guard[0])
        pe_x[c] = R.c_pe.tag()
        S.add("pe", lambda E, s=xin[sl], p=p_sum, m=R.c_pe.sem, st=(c == 0), sp_=(c == cfg.C - 1):
              E.matmul(p[0:1, :], R.sb.ones_col[:, :], s[:, :],
                       start=st, stop=sp_).then_inc(m, 1))
        S.wait("pe", R.c_act.sem, act_x[c])
        if c == 0:
            S.wait("pe", R.c_act.sem, bank_guard[1])
        pe_q[c] = R.c_pe.tag()
        S.add("pe", lambda E, s=sq[sl], p=p_sq, m=R.c_pe.sem, st=(c == 0), sp_=(c == cfg.C - 1):
              E.matmul(p[0:1, :], R.sb.ones_col[:, :], s[:, :],
                       start=st, stop=sp_).then_inc(m, 1))
    xin_free = [(R.c_pe.sem, pe_x[cfg.C - 2]), (R.c_pe.sem, pe_x[cfg.C - 1]),
                (R.c_act.sem, act_x[cfg.C - 2]), (R.c_act.sem, act_x[cfg.C - 1])]
    sq_free = [(R.c_pe.sem, pe_q[cfg.C - 2]), (R.c_pe.sem, pe_q[cfg.C - 1])]
    return (pe_x[cfg.C - 1], pe_q[cfg.C - 1]), xin_free, sq_free


def emit_normalize_stream(S, R, src_dram, half, ab, ab_ready, store_toks=None,
                          prev_waits=()):
    cfg = R.cfg
    A, B = ab
    xin = R.sb.xin
    dve_n = [None] * cfg.C
    for c in range(cfg.C):
        sl = c % 2
        if c == 0:
            for sem_, val_ in prev_waits:
                S.wait("sp", sem_, val_)
        if store_toks is not None:
            S.wait_tok("sp", store_toks[c])
        if c >= 2:
            S.wait("sp", R.c_dve.sem, dve_n[c - 2])
        tok = R.ch_x.tag(sl)
        src_ap = src_dram[c * 128:(c + 1) * 128, half * cfg.TH:(half + 1) * cfg.TH]
        S.add("sp", lambda E, d=xin[sl], s=src_ap, m=tok[0]:
              E.dma_start(out=d[:, :], in_=s).then_inc(m, 16))
        S.wait_tok("dve", tok)
        if c == 0:
            for sem_, val_ in ab_ready:
                S.wait("dve", sem_, val_)
        S.add("dve", lambda E, s=xin[sl], a=A:
              E.tensor_tensor(R.sb.tmp[:, :], s[:, :], a[:, :], ALU.mult))
        dve_n[c] = R.c_dve.tag()
        S.add("dve", lambda E, c_=c, b=B, m=R.c_dve.sem: E.tensor_tensor(
            R.sb.y[:, c_, :], R.sb.tmp[:, :], b[:, :], ALU.add).then_inc(m, 1))
    R.ab_free = [(R.c_dve.sem, dve_n[cfg.C - 1])]
    return dve_n[cfg.C - 1], [(R.c_dve.sem, dve_n[cfg.C - 2]),
                              (R.c_dve.sem, dve_n[cfg.C - 1])]


def emit_debug_dump(S, R, sbuf_src, dram_dst, gate_ctr, gate_tok):
    S.wait(("sp"), gate_ctr.sem, gate_tok)
    tok = R.ch_dbg.tag(0)
    S.add("sp", lambda E, d=dram_dst, s=sbuf_src, m=tok[0]:
          E.dma_start(out=d, in_=s).then_inc(m, 16))
    S.wait_tok("sp", tok)
    S.wait_tok("dve", tok)
    S.wait_tok("act", tok)


def emit_ffn(S, R, app, half, y_ready, xin_prev=(), sq_prev=()):
    cfg = R.cfg
    sb = R.sb
    y, h = sb.y, sb.h
    resid_src = R.dr.x_fm if app == 1 else R.dr.x2_save
    out_dst = R.dr.x1_save if app == 1 else R.dr.out_fm

    pg = [R.ps0, R.ps1]
    pu = [R.ps2, R.ps3]
    pe_g = [None] * cfg.IC
    pe_u = [None] * cfg.IC
    act_g = [None] * cfg.IC
    act_u = [None] * cfg.IC
    dve_h = [None] * cfg.IC
    for n in range(cfg.IC):
        sl = n % 2
        if n >= 2:
            S.wait("gp", R.c_pe.sem, pe_g[n - 2])
            S.wait("gp", R.c_pe.sem, pe_u[n - 2])
        tg = R.ch_wg.tag(sl)
        S.add("gp", lambda E, d=sb.wg2[sl], s=R.dr.wg_hb[n, :, :, :], m=tg[0]:
              E.dma_start(out=d[:, :, :], in_=s).then_inc(m, 16))
        tu = R.ch_wu.tag(sl)
        S.add("gp", lambda E, d=sb.wu2[sl], s=R.dr.wu_hb[n, :, :, :], m=tu[0]:
              E.dma_start(out=d[:, :, :], in_=s).then_inc(m, 16))
        S.wait_tok("pe", tg)
        if n == 0:
            S.wait("pe", R.c_dve.sem, y_ready)
        if n >= 2:
            S.wait("pe", R.c_act.sem, act_g[n - 2])
        for c in range(cfg.C):
            if c == cfg.C - 1:
                pe_g[n] = R.c_pe.tag()
                S.add("pe", lambda E, p=pg[sl], w=sb.wg2[sl], c_=c, m=R.c_pe.sem:
                      E.matmul(p[:, :], w[:, c_, :], y[:, c_, :],
                               start=False, stop=True).then_inc(m, 1))
            else:
                S.add("pe", lambda E, p=pg[sl], w=sb.wg2[sl], c_=c, st=(c == 0):
                      E.matmul(p[:, :], w[:, c_, :], y[:, c_, :], start=st, stop=False))
        S.wait_tok("pe", tu)
        if n >= 2:
            S.wait("pe", R.c_act.sem, act_u[n - 2])
        for c in range(cfg.C):
            if c == cfg.C - 1:
                pe_u[n] = R.c_pe.tag()
                S.add("pe", lambda E, p=pu[sl], w=sb.wu2[sl], c_=c, m=R.c_pe.sem:
                      E.matmul(p[:, :], w[:, c_, :], y[:, c_, :],
                               start=False, stop=True).then_inc(m, 1))
            else:
                S.add("pe", lambda E, p=pu[sl], w=sb.wu2[sl], c_=c, st=(c == 0):
                      E.matmul(p[:, :], w[:, c_, :], y[:, c_, :], start=st, stop=False))
        S.wait("act", R.c_pe.sem, pe_g[n])
        act_g[n] = R.c_act.tag()
        S.add("act", lambda E, p=pg[sl], n_=n, m=R.c_act.sem: E.activation(
            h[:, n_, :], p[:, :], AF.Silu, bias=0.0, scale=sb.scl_b[:, 0:1]).then_inc(m, 1))
        S.wait("act", R.c_pe.sem, pe_u[n])
        if n >= 2:
            S.wait("act", R.c_dve.sem, dve_h[n - 2])
        act_u[n] = R.c_act.tag()
        S.add("act", lambda E, p=pu[sl], d=sb.ustage[sl], m=R.c_act.sem: E.activation(
            d[:, :], p[:, :], AF.Copy, bias=0.0, scale=sb.scl_b[:, 1:2]).then_inc(m, 1))
        S.wait("dve", R.c_act.sem, act_g[n])
        S.wait("dve", R.c_act.sem, act_u[n])
        dve_h[n] = R.c_dve.tag()
        S.add("dve", lambda E, n_=n, u=sb.ustage[sl], m=R.c_dve.sem: E.tensor_tensor(
            h[:, n_, :], h[:, n_, :], u[:, :], ALU.mult).then_inc(m, 1))

    pd = [R.ps0, R.ps1]
    xin = R.sb.xin
    pe_d = [None] * cfg.C
    dve_d = [None] * cfg.C
    st_tok = [None] * cfg.C
    wd_cons = [None] * (2 * cfg.C)
    wd_toks = {}

    for m_ in range(cfg.C):
        sl = m_ % 2
        for j in range(2):
            idx = 2 * m_ + j
            if idx >= 2:
                S.wait("gp", R.c_pe.sem, wd_cons[idx - 2])
            td = R.ch_wd.tag(j)
            wd_toks[idx] = td
            S.add("gp", lambda E, d=sb.wd2[j], s=R.dr.wd_hb[m_, :, j * cfg.ICH:(j + 1) * cfg.ICH, :],
                  m=td[0]: E.dma_start(out=d[:, :, :], in_=s).then_inc(m, 16))
        if m_ == 0:
            for sem_, val_ in xin_prev:
                S.wait("sp", sem_, val_)
        if m_ >= 2:
            S.wait("sp", R.c_dve.sem, dve_d[m_ - 2])
        tr = R.ch_x.tag(sl)
        src_ap = resid_src[m_ * 128:(m_ + 1) * 128, half * cfg.TH:(half + 1) * cfg.TH]
        S.add("sp", lambda E, d=xin[sl], s=src_ap, m=tr[0]:
              E.dma_start(out=d[:, :], in_=s).then_inc(m, 16))
        if m_ == 0:
            S.wait("pe", R.c_dve.sem, dve_h[cfg.IC - 1])
            S.wait("pe", R.c_act.sem, act_g[cfg.IC - 1])
            S.wait("pe", R.c_act.sem, act_g[cfg.IC - 2])
        if m_ >= 2:
            S.wait("pe", R.c_dve.sem, dve_d[m_ - 2])
        for j in range(2):
            idx = 2 * m_ + j
            S.wait_tok("pe", wd_toks[idx])
            for nn in range(cfg.ICH):
                n = j * cfg.ICH + nn
                first = n == 0
                last = n == cfg.IC - 1
                grp_last = nn == cfg.ICH - 1
                if grp_last:
                    tokp = R.c_pe.tag()
                    wd_cons[idx] = tokp
                    if last:
                        pe_d[m_] = tokp
                    S.add("pe", lambda E, p=pd[sl], w=sb.wd2[j], nn_=nn, n_=n, st=first,
                          sp_=last, m=R.c_pe.sem: E.matmul(
                        p[:, :], w[:, nn_, :], h[:, n_, :], start=st, stop=sp_).then_inc(m, 1))
                else:
                    S.add("pe", lambda E, p=pd[sl], w=sb.wd2[j], nn_=nn, n_=n, st=first:
                          E.matmul(p[:, :], w[:, nn_, :], h[:, n_, :], start=st, stop=False))
        S.wait("dve", R.c_pe.sem, pe_d[m_])
        S.wait_tok("dve", tr)
        if m_ >= 2:
            S.wait_tok("dve", st_tok[m_ - 2])
        S.add("dve", lambda E, p=pd[sl], d=sb.tmpd[sl]: E.tensor_scalar(
            d[:, :], p[:, :], sb.scl_b[:, 3:4], None, ALU.mult))
        dve_d[m_] = R.c_dve.tag()
        S.add("dve", lambda E, s=xin[sl], d=sb.x1stage[sl], t=sb.tmpd[sl], m=R.c_dve.sem:
              E.tensor_tensor(d[:, :], t[:, :], s[:, :], ALU.add).then_inc(m, 1))
        S.wait("sp", R.c_dve.sem, dve_d[m_])
        stok = R.ch_sx1.tag(sl)
        st_tok[m_] = stok
        dst_ap = out_dst[m_ * 128:(m_ + 1) * 128, half * cfg.TH:(half + 1) * cfg.TH]
        S.add("sp", lambda E, d=dst_ap, s=sb.x1stage[sl], m=stok[0]:
              E.dma_start(out=d, in_=s[:, :]).then_inc(m, 16))

    if app == 1:
        down_xin_free = [(R.c_dve.sem, dve_d[cfg.C - 2]), (R.c_dve.sem, dve_d[cfg.C - 1])]
        sqp = list(sq_prev) + [(R.c_dve.sem, dve_d[cfg.C - 2]), (R.c_dve.sem, dve_d[cfg.C - 1])]
        stats, xin_free2, sq_free2 = emit_ln_stats_stream(
            S, R, R.dr.x1_save, half,
            bank_guard=(act_u[cfg.IC - 2], act_u[cfg.IC - 1]),
            store_toks=st_tok, prev_waits=down_xin_free, sq_prev=sqp)
        R.ffn_sq_free = sq_free2
        R.ab_free = (R.ab_free or []) + [st_tok[cfg.C - 2], st_tok[cfg.C - 1],
                                         (R.c_dve.sem, dve_d[cfg.C - 1])]
        if getattr(R.cfg, "debug", False) and half == 0:
            S.wait("act", R.c_pe.sem, stats[0])
            S.wait("act", R.c_pe.sem, stats[1])
            tdbg = R.c_act.tag()
            S.add("act", lambda E, m=R.c_act.sem: E.copy(
                R.sb.dbg_rows_sb[0:1, :], R.ps2[0:1, :]).then_inc(m, 1))
            S.wait("sp", R.c_act.sem, tdbg)
            tokd = R.ch_dbg.tag(0)
            S.add("sp", lambda E, m=tokd[0]: E.dma_start(
                out=R.dr.dbg_rows[0:1, 0:R.cfg.TH], in_=R.sb.dbg_rows_sb[0:1, :]).then_inc(m, 16))
            S.wait_tok("sp", tokd)
            S.wait("act", R.ch_dbg.sems[0], tokd[1])
            tdbg = R.c_act.tag()
            S.add("act", lambda E, m=R.c_act.sem: E.copy(
                R.sb.dbg_rows_sb[0:1, :], R.ps3[0:1, :]).then_inc(m, 1))
            S.wait("sp", R.c_act.sem, tdbg)
            tokd = R.ch_dbg.tag(0)
            S.add("sp", lambda E, m=tokd[0]: E.dma_start(
                out=R.dr.dbg_rows[0:1, R.cfg.TH:2 * R.cfg.TH], in_=R.sb.dbg_rows_sb[0:1, :]).then_inc(m, 16))
            S.wait_tok("sp", tokd)
            S.wait("pe", R.c_act.sem, tdbg)
            S.wait("dve", R.c_act.sem, tdbg)
        ab_ready = emit_ln_rows_and_bcast(
            S, R, R.ps2, R.ps3, stats, (sb.A, sb.B))
        if getattr(R.cfg, "debug", False) and half == 0:
            S.wait_tok("sp", ab_ready[0])
            S.wait_tok("sp", ab_ready[1])
            tokd = R.ch_dbg.tag(0)
            S.add("sp", lambda E, m=tokd[0]: E.dma_start(
                out=R.dr.dbg_ab[:, 0:cfg.TH], in_=sb.A[:, :]).then_inc(m, 16))
            tokd2 = R.ch_dbg.tag(0)
            S.add("sp", lambda E, m=tokd2[0]: E.dma_start(
                out=R.dr.dbg_ab[:, cfg.TH:2 * cfg.TH], in_=sb.B[:, :]).then_inc(m, 16))
            S.wait_tok("sp", tokd2)
            tokd3 = R.ch_dbg.tag(0)
            S.add("sp", lambda E, m=tokd3[0]: E.dma_start(
                out=R.dr.dbg_rows2[0:1, 0:cfg.TH], in_=R.sb.rows[2][0:1, :]).then_inc(m, 16))
            tokd4 = R.ch_dbg.tag(0)
            S.add("sp", lambda E, m=tokd4[0]: E.dma_start(
                out=R.dr.dbg_rows2[0:1, cfg.TH:2 * cfg.TH], in_=R.sb.rows[3][0:1, :]).then_inc(m, 16))
            S.wait_tok("sp", tokd4)
        last_norm, _ = emit_normalize_stream(
            S, R, R.dr.x1_save, half, (sb.A, sb.B), ab_ready,
            prev_waits=xin_free2)
        S.wait("sp", R.c_dve.sem, last_norm)
        ytok = R.ch_y2w.tag(0)
        dst = R.dr.y2_cc_ap.rearrange("(c p) t -> p c t", p=128)[
            :, :, half * cfg.TH:(half + 1) * cfg.TH]
        S.add("sp", lambda E, d=dst, m=ytok[0]:
              E.dma_start(out=d, in_=sb.y[:, :, :]).then_inc(m, 16))
        R.y2_store_toks.append(ytok)
    else:
        S.wait_tok("sp", st_tok[cfg.C - 2])
        S.wait_tok("sp", st_tok[cfg.C - 1])


def emit_collective(S, R, which):
    if which == 1:
        for t in R.y2_store_toks:
            S.wait_tok("gp", t)
        i_h, o_h = R.dr.y2_cc_h, R.dr.y2_gath_h
    else:
        S.wait_tok("gp", R.ch_ao.total(0))
        i_h, o_h = R.dr.ao_cc_h, R.dr.ao_gath_h
    tcc = R.c_cc.tag()
    S.add("gp", lambda E, i=i_h, o=o_h, m=R.c_cc.sem: E.collective_compute(
        "AllGather", ALU.bypass,
        replica_groups=[[0, 1, 2, 3], [4, 5, 6, 7]],
        ins=[i.ap().opt()], outs=[o.ap().opt()]).then_inc(m, 1))
    S.wait("gp", R.c_cc.sem, tcc)
    return tcc
    if getattr(R.cfg, "debug", False):
        dbg_dst = R.dr.dbg_y2cc if which == 1 else R.dr.dbg_aocc
        tok = R.ch_dbg.tag(0)
        S.add("gp", lambda E, d=dbg_dst, s=o_h.ap(), m=tok[0]:
              E.dma_start(out=d, in_=s).then_inc(m, 16))
        S.wait_tok("gp", tok)
        if which == 2:
            tok2 = R.ch_dbg.tag(0)
            S.add("gp", lambda E, d=R.dr.dbg_aoctr, s=i_h.ap(), m=tok2[0]:
                  E.dma_start(out=d, in_=s).then_inc(m, 16))
            S.wait_tok("gp", tok2)


def emit_attention(S, R):
    cfg = R.cfg
    sb = R.sb
    R.ch_cs.tag(0)
    S.add("sp", lambda E, m=R.ch_cs.sems[0]: E.dma_start(
        out=sb.cos_t[:, :], in_=R.dr.cos_t[:, :]).then_inc(m, 16))
    tc2 = R.ch_cs.tag(0)
    S.add("sp", lambda E, m=tc2[0]: E.dma_start(
        out=sb.sin_t[:, :], in_=R.dr.sin_t[:, :]).then_inc(m, 16))
    S.wait_tok("dve", tc2)
    prev_head_done = None
    for pair in range(cfg.PAIRS):
        hA, hB = 2 * pair, 2 * pair + 1
        half_c = cfg.C // 2
        wq_toks = []
        for wslot in range(2):
            S.wait("gp", R.c_pe.sem, R.wq_free[wslot])
            tok = R.ch_wq.tag(wslot)
            src = R.dr.wqkv[:, pair * 768:(pair + 1) * 768].rearrange(
                "(c p) j -> p c j", p=128)[:, wslot * half_c:(wslot + 1) * half_c, :]
            S.add("gp", lambda E, d=sb.wq[:, wslot * half_c:(wslot + 1) * half_c, :],
                  s=src, m=tok[0]: E.dma_start(out=d, in_=s).then_inc(m, 16))
            wq_toks.append(tok)
        if pair == 0 and getattr(R, "coll1_hook", None):
            R.coll1_hook(S)
        t_bqs = None
        for i, hh in enumerate((hA, hB)):
            t_bqs = R.c_dve.tag()
            S.add("dve", lambda E, i_=i, hh_=hh, m=R.c_dve.sem: E.tensor_scalar(
                sb.bqs[:, i_:i_ + 1], sb.bq[:, 3 * hh_:3 * hh_ + 1], QSC, None,
                ALU.mult).then_inc(m, 1))

        pq = [R.aq[0], R.aq[1]]
        pk = [R.aq[2], R.aq[3]]
        pv = [R.aq[4], R.aq[5]]
        ptv = R.atr[0]
        QX, KX, VX = [sb.Qa, sb.Qb], [sb.Ka, sb.Kb], [sb.Va, sb.Vb]
        rope_last = [None, None]
        vtok_last = [None, None]
        evac_last = [[None, None, None], [None, None, None]]
        for tg in range(cfg.TGN):
            blk_i, colh = tg // 2, tg % 2
            pe_last = {}
            for cg in range(cfg.CGN):
                gidx = (pair * cfg.TGN + tg) * cfg.CGN + cg
                ysl = gidx % 2
                if gidx == 0 and getattr(R, "cc1_tok", None):
                    S.wait("sp", R.c_cc.sem, R.cc1_tok)
                S.wait("sp", R.c_pe.sem, R.y2_cons.get(gidx - 2))
                tok = R.ch_y2r.tag(ysl)
                rows = R.dr.y2_gath[
                    blk_i * cfg.D + cg * cfg.CG * 128:
                    blk_i * cfg.D + (cg + 1) * cfg.CG * 128,
                    colh * cfg.TH:(colh + 1) * cfg.TH]
                src = rows.rearrange("(c p) t -> p c t", p=128)
                S.add("sp", lambda E, d=sb.y2sb[ysl], s=src, m=tok[0]:
                      E.dma_start(out=d[:, :, :], in_=s).then_inc(m, 16))
                S.wait_tok("pe", tok)
                if cg == 0:
                    S.wait_tok("pe", wq_toks[0])
                    if cfg.CGN == 1:
                        S.wait_tok("pe", wq_toks[1])
                    if tg == 0:
                        S.wait("pe", R.c_dve.sem, prev_head_done)
                    for i in range(2):
                        for kk in range(3):
                            S.wait("pe", R.c_act.sem, evac_last[i][kk])
                elif cg == (cfg.CGN + 1) // 2:
                    S.wait_tok("pe", wq_toks[1])
                for c8 in range(cfg.CG):
                    c = cg * cfg.CG + c8
                    first = c == 0
                    last = c == cfg.C - 1
                    cg_final = c8 == cfg.CG - 1
                    for i in range(2):
                        for kk in range(3):
                            pp = (pq, pk, pv)[kk]
                            jcol = i * 384 + kk * 128
                            tag_it = last or (cg_final and i == 1 and kk == 2)
                            if tag_it:
                                tokm = R.c_pe.tag()
                                if last:
                                    pe_last[(i, kk)] = tokm
                                if cg_final and i == 1 and kk == 2:
                                    R.y2_cons[gidx] = tokm
                                S.add("pe", lambda E, p=pp[i], c_=c, j=jcol, ysl_=ysl,
                                      c8_=c8, st=first, sp_=last, m=R.c_pe.sem: E.matmul(
                                    p[:, 0:cfg.TH], sb.wq[:, c_, j:j + 128],
                                    sb.y2sb[ysl_][:, c8_, :],
                                    start=st, stop=sp_).then_inc(m, 1))
                            else:
                                S.add("pe", lambda E, p=pp[i], c_=c, j=jcol, ysl_=ysl,
                                      c8_=c8, st=first: E.matmul(
                                    p[:, 0:cfg.TH], sb.wq[:, c_, j:j + 128],
                                    sb.y2sb[ysl_][:, c8_, :], start=st, stop=False))
            for i in range(2):
                hh = (hA, hB)[i]
                S.wait("act", R.c_pe.sem, pe_last[(i, 0)])
                S.wait("act", R.c_dve.sem, t_bqs)
                S.wait("act", R.c_dve.sem, rope_last[i])
                evac_last[i][0] = R.c_act.tag()
                S.add("act", lambda E, p=pq[i], i_=i, m=R.c_act.sem: E.activation(
                    sb.qf[:, :], p[:, 0:cfg.TH], AF.Identity, bias=sb.bqs[:, i_:i_ + 1],
                    scale=QSC).then_inc(m, 1))
                S.add("act", lambda E: E.copy(sb.rot[0:64, :], sb.qf[64:128, :]))
                t_rot = R.c_act.tag()
                S.add("act", lambda E, m=R.c_act.sem:
                      E.copy(sb.rot[64:128, :], sb.qf[0:64, :]).then_inc(m, 1))
                if getattr(R.cfg, "debug", False) and pair == 0 and i == 0 and tg == 0:
                    S.wait("sp", R.c_act.sem, t_rot)
                    for col0, srct in ((0, sb.qf), (512, sb.rot),
                                       (1024, sb.cos_t), (1536, sb.sin_t)):
                        tokd = R.ch_dbg.tag(0)
                        S.add("sp", lambda E, c0=col0, s_=srct, m=tokd[0]: E.dma_start(
                            out=R.dr.dbg_p[:, c0:c0 + 512], in_=s_[:, 0:512]).then_inc(m, 16))
                        S.wait_tok("sp", tokd)
                    S.wait("dve", R.ch_dbg.sems[0], R.ch_dbg.cnt[0] * 16)
                S.wait("dve", R.c_act.sem, t_rot)
                S.add("dve", lambda E, tg_=tg: E.tensor_tensor(
                    sb.t1[:, :], sb.qf[:, :],
                    sb.cos_t[:, tg_ * cfg.TH:(tg_ + 1) * cfg.TH], ALU.mult))
                S.add("dve", lambda E, tg_=tg: E.tensor_tensor(
                    sb.t2[:, :], sb.rot[:, :],
                    sb.sin_t[:, tg_ * cfg.TH:(tg_ + 1) * cfg.TH], ALU.mult))
                tq_done = R.c_dve.tag()
                S.add("dve", lambda E, d=QX[i], tg_=tg, m=R.c_dve.sem: E.tensor_tensor(
                    d[:, tg_ * cfg.TH:(tg_ + 1) * cfg.TH], sb.t1[:, :], sb.t2[:, :],
                    ALU.add).then_inc(m, 1))
                S.wait("act", R.c_pe.sem, pe_last[(i, 1)])
                S.wait("act", R.c_dve.sem, tq_done)
                evac_last[i][1] = R.c_act.tag()
                S.add("act", lambda E, p=pk[i], hh_=hh, m=R.c_act.sem: E.activation(
                    sb.qf[:, :], p[:, 0:cfg.TH], AF.Identity,
                    bias=sb.bq[:, 3 * hh_ + 1:3 * hh_ + 2], scale=1.0).then_inc(m, 1))
                S.add("act", lambda E: E.copy(sb.rot[0:64, :], sb.qf[64:128, :]))
                t_rot2 = R.c_act.tag()
                S.add("act", lambda E, m=R.c_act.sem:
                      E.copy(sb.rot[64:128, :], sb.qf[0:64, :]).then_inc(m, 1))
                S.wait("dve", R.c_act.sem, t_rot2)
                S.add("dve", lambda E, tg_=tg: E.tensor_tensor(
                    sb.t1[:, :], sb.qf[:, :],
                    sb.cos_t[:, tg_ * cfg.TH:(tg_ + 1) * cfg.TH], ALU.mult))
                S.add("dve", lambda E, tg_=tg: E.tensor_tensor(
                    sb.t2[:, :], sb.rot[:, :],
                    sb.sin_t[:, tg_ * cfg.TH:(tg_ + 1) * cfg.TH], ALU.mult))
                rope_last[i] = R.c_dve.tag()
                S.add("dve", lambda E, d=KX[i], tg_=tg, m=R.c_dve.sem: E.tensor_tensor(
                    d[:, tg_ * cfg.TH:(tg_ + 1) * cfg.TH], sb.t1[:, :], sb.t2[:, :],
                    ALU.add).then_inc(m, 1))
                S.wait("act", R.c_pe.sem, pe_last[(i, 2)])
                S.wait("act", R.c_dve.sem, vtok_last[i])
                evac_last[i][2] = R.c_act.tag()
                S.add("act", lambda E, p=pv[i], hh_=hh, m=R.c_act.sem: E.activation(
                    sb.vstage[:, :], p[:, 0:cfg.TH], AF.Identity,
                    bias=sb.bq[:, 3 * hh_ + 2:3 * hh_ + 3], scale=1.0).then_inc(m, 1))
                S.wait("pe", R.c_act.sem, evac_last[i][2])
                ntile = cfg.TH // 128
                for jj in range(ntile):
                    gg = R.tr_gg
                    R.tr_gg += 1
                    bank = R.atr[gg % 2]
                    if gg >= 2:
                        S.wait("pe", R.c_dve.sem, R.tr_hist[gg - 2])
                    ttr = R.c_pe.tag()
                    S.add("pe", lambda E, jj_=jj, p=bank, m=R.c_pe.sem: E.transpose(
                        p[:, 0:128], sb.vstage[:, jj_ * 128:(jj_ + 1) * 128],
                        sb.ident[:, :]).then_inc(m, 1))
                    S.wait("dve", R.c_pe.sem, ttr)
                    tev = R.c_dve.tag()
                    S.add("dve", lambda E, d=VX[i], c0=tg * cfg.TH + jj * 128, p=bank,
                          m=R.c_dve.sem: E.tensor_copy(
                        d[:, c0:c0 + 128], p[:, 0:128]).then_inc(m, 1))
                    R.tr_hist.append(tev)
                vtok_last[i] = R.tr_hist[-1]
        R.wq_free[0] = pe_last[(1, 2)]
        R.wq_free[1] = pe_last[(1, 2)]
        if getattr(R.cfg, "debug", False) and pair == 0:
            S.wait("sp", R.c_dve.sem, rope_last[0])
            S.wait("sp", R.c_dve.sem, rope_last[1])
            S.wait("sp", R.c_dve.sem, vtok_last[0])
            for dst, srct in ((R.dr.dbg_q, sb.Qa), (R.dr.dbg_k, sb.Ka), (R.dr.dbg_v, sb.Va)):
                tokd = R.ch_dbg.tag(0)
                S.add("sp", lambda E, d=dst, s=srct, m=tokd[0]:
                      E.dma_start(out=d, in_=s[:, :]).then_inc(m, 16))
                S.wait_tok("sp", tokd)

        for i in range(2):
            hh = (hA, hB)[i]
            deps = [rope_last[0], rope_last[1], vtok_last[0], vtok_last[1], prev_head_done]
            prev_head_done = emit_head_attention(S, R, QX[i], KX[i], VX[i], hh, deps)


def emit_head_attention(S, R, Q, K, V, hh, dve_deps):
    cfg = R.cfg
    sb = R.sb
    ps = [R.aq[0], R.aq[1]]
    po = [R.aq[4], R.aq[5]]
    seg_id_box = [0]
    seg_evac = {}
    max_tok = [None] * cfg.NQ
    exp_tok = [None] * cfg.NQ
    dve_p = [None] * cfg.NQ
    av_done = [None] * cfg.NQ
    out_evac = [None] * cfg.NQ
    first_box = [True]

    def emit_scores(qi):
        kw = (qi + 1) * 128
        nseg = (kw + 511) // 512
        for s_ in range(nseg):
            w = min(512, kw - s_ * 512)
            if first_box[0]:
                for d in dve_deps:
                    S.wait("pe", R.c_dve.sem, d)
                first_box[0] = False
            seg_id = seg_id_box[0]
            seg_id_box[0] += 1
            if seg_id >= 2:
                S.wait("pe", R.c_act.sem, seg_evac[seg_id - 2])
            tokm = R.c_pe.tag()
            S.add("pe", lambda E, p=ps[seg_id % 2], q0=qi * 128, s0=s_ * 512, w_=w,
                  m=R.c_pe.sem: E.matmul(
                p[:, 0:w_], Q[:, q0:q0 + 128], K[:, s0:s0 + w_],
                start=True, stop=True).then_inc(m, 1))
            S.wait("act", R.c_pe.sem, tokm)
            if qi >= 1 and s_ == 0:
                S.wait("act", R.c_dve.sem, max_tok[qi - 1])
            ev = R.c_act.tag()
            seg_evac[seg_id] = ev
            S.add("act", lambda E, p=ps[seg_id % 2], s0=s_ * 512, w_=w, m=R.c_act.sem:
                  E.copy(sb.scores[:, s0:s0 + w_], p[:, 0:w_]).then_inc(m, 1))

    def emit_softmax(qi):
        kw = (qi + 1) * 128
        P = sb.P[qi % 2]
        S.wait("dve", R.c_act.sem, seg_evac[seg_id_box[0] - 1])
        max_tok[qi] = R.c_dve.tag()
        S.add("dve", lambda E, kw_=kw, m=R.c_dve.sem: E.reduce_max(
            sb.negmax[:, :], sb.scores[:, 0:kw_], axis=AX.X, negate=True).then_inc(m, 1))
        S.wait("act", R.c_dve.sem, max_tok[qi])
        if qi >= 2:
            S.wait("act", R.c_pe.sem, av_done[qi - 2])
        exp_tok[qi] = R.c_act.tag()
        S.add("act", lambda E, kw_=kw, P_=P, m=R.c_act.sem: E.activation(
            P_[:, 0:kw_], sb.scores[:, 0:kw_], AF.Exp, bias=sb.negmax[:, 0:1],
            scale=1.0).then_inc(m, 1))
        S.wait("dve", R.c_act.sem, exp_tok[qi])
        S.add("dve", lambda E, q0=qi * 128, kw_=kw, P_=P: E.tensor_tensor(
            P_[:, q0:kw_], P_[:, q0:kw_], sb.mask01[:, :], ALU.mult))
        sum_tok = R.c_dve.tag()
        S.add("dve", lambda E, kw_=kw, P_=P, m=R.c_dve.sem: E.reduce_sum(
            sb.lrow[:, :], P_[:, 0:kw_], axis=AX.X).then_inc(m, 1))
        S.wait("act", R.c_dve.sem, sum_tok)
        S.add("act", lambda E: E.activation(sb.rrow[:, :], sb.lrow[:, :], AF.Ln))
        rec_tok = R.c_act.tag()
        S.add("act", lambda E, m=R.c_act.sem: E.activation(
            sb.rrow[:, :], sb.rrow[:, :], AF.Exp, bias=0.0, scale=-1.0).then_inc(m, 1))
        S.wait("dve", R.c_act.sem, rec_tok)
        dve_p[qi] = R.c_dve.tag()
        S.add("dve", lambda E, kw_=kw, P_=P, m=R.c_dve.sem: E.tensor_scalar(
            P_[:, 0:kw_], P_[:, 0:kw_], sb.rrow[:, 0:1], None, ALU.mult).then_inc(m, 1))

    def emit_tav(qi):
        P = sb.P[qi % 2]
        S.wait("pe", R.c_dve.sem, dve_p[qi])
        if qi >= 2:
            S.wait("pe", R.c_dve.sem, out_evac[qi - 2])
        av_tok = [None] * (qi + 1)
        pt_tok = [None] * (qi + 1)

        def emit_av(kt):
            S.wait("pe", R.c_dve.sem, pt_tok[kt])
            tok = R.c_pe.tag()
            av_tok[kt] = tok
            if kt == qi:
                av_done[qi] = tok
            S.add("pe", lambda E, p=po[qi % 2], kt_=kt, st=(kt == 0), sp2=(kt == qi),
                  m=R.c_pe.sem: E.matmul(
                p[:, 0:128], V[:, kt_ * 128:(kt_ + 1) * 128],
                sb.PT[kt_ % 2][:, 0:128], start=st, stop=sp2).then_inc(m, 1))

        for kt in range(qi + 1):
            gg = R.tr_gg
            R.tr_gg += 1
            bank = R.atr[gg % 2]
            if gg >= 2:
                S.wait("pe", R.c_dve.sem, R.tr_hist[gg - 2])
            ttr = R.c_pe.tag()
            S.add("pe", lambda E, p=bank, kt_=kt, P_=P, m=R.c_pe.sem: E.transpose(
                p[:, 0:128], P_[:, kt_ * 128:(kt_ + 1) * 128],
                sb.ident[:, :]).then_inc(m, 1))
            if kt >= 1:
                emit_av(kt - 1)
            S.wait("dve", R.c_pe.sem, ttr)
            if kt >= 2:
                S.wait("dve", R.c_pe.sem, av_tok[kt - 2])
            pt_tok[kt] = R.c_dve.tag()
            S.add("dve", lambda E, p=bank, kt_=kt, m=R.c_dve.sem: E.tensor_copy(
                sb.PT[kt_ % 2][:, 0:128], p[:, 0:128]).then_inc(m, 1))
            R.tr_hist.append(pt_tok[kt])
        emit_av(qi)
        S.wait("dve", R.c_pe.sem, av_done[qi])
        if qi == 0 and R.ao_store_toks:
            S.wait_tok("dve", R.ao_store_toks[-1])
        out_evac[qi] = R.c_dve.tag()
        S.add("dve", lambda E, p=po[qi % 2], q0=qi * 128, m=R.c_dve.sem:
              E.tensor_copy(sb.outT[:, q0:q0 + 128], p[:, 0:128]).then_inc(m, 1))

    emit_scores(0)
    for qi in range(cfg.NQ):
        emit_softmax(qi)
        if qi + 1 < cfg.NQ:
            emit_scores(qi + 1)
        emit_tav(qi)
    S.wait("sp", R.c_dve.sem, out_evac[cfg.NQ - 1])
    stok = R.ch_ao.tag(0)
    dst = R.dr.ao_cc[hh * 128:(hh + 1) * 128, :]
    S.add("sp", lambda E, d=dst, m=stok[0]:
          E.dma_start(out=d, in_=sb.outT[:, :]).then_inc(m, 16))
    R.ao_store_toks.append(stok)
    return out_evac[cfg.NQ - 1]


def emit_proj(S, R):
    cfg = R.cfg
    sb = R.sb
    wp_pre = {}
    for m0 in range(2):
        twp = R.ch_wp.tag(m0)
        wp_pre[m0] = twp
        srcw = R.dr.wproj[:, m0 * 128:(m0 + 1) * 128].rearrange("(c p) i -> p c i", p=128)
        S.add("gp", lambda E, d=sb.wp2[m0], s=srcw, m=twp[0]:
              E.dma_start(out=d[:, :, :], in_=s).then_inc(m, 16))
    emit_collective(S, R, 2)
    ag_toks = [R.ch_ag.tag(0), R.ch_ag.tag(1)]

    def _gp_ag(E, R=R, cfg=cfg):
        with E.register("agoff") as rg:
            for half in range(2):
                E.reg_load(rg, R.sb.blkoff[0:1, 0:1])
                if half:
                    E.reg_add(rg, rg, cfg.TH)
                ap = bass.AP(R.dr.ao_gath_h, rg,
                             [[cfg.S, 128], [128 * cfg.S, cfg.C], [1, cfg.TH]])
                E.dma_start(out=R.sb.agh[half][:, :, :], in_=ap).then_inc(
                    R.ch_ag.sems[half], 16)
    S.add("gp", _gp_ag)

    pm = [[R.ps0, R.ps1], [R.ps2, R.ps3]]
    pe_m = [[None] * cfg.C for _ in range(2)]
    dve_m = [[None] * cfg.C for _ in range(2)]
    act_e = [[None] * cfg.C for _ in range(2)]
    st_tok = [[None] * cfg.C for _ in range(2)]

    for m_ in range(cfg.C):
        sl = m_ % 2
        if m_ < 2:
            twp = wp_pre[m_]
        else:
            S.wait("gp", R.c_pe.sem, pe_m[1][m_ - 2])
            twp = R.ch_wp.tag(sl)
            src = R.dr.wproj[:, m_ * 128:(m_ + 1) * 128].rearrange("(c p) i -> p c i", p=128)
            S.add("gp", lambda E, d=sb.wp2[sl], s=src, m=twp[0]:
                  E.dma_start(out=d[:, :, :], in_=s).then_inc(m, 16))
        for half in range(2):
            ch_res = R.ch_x1r if half == 0 else R.ch_x
            ch_st = R.ch_sx2 if half == 0 else R.ch_sx1
            if m_ >= 2:
                S.wait("sp", R.c_dve.sem, dve_m[half][m_ - 2])
            tx = ch_res.tag(sl)
            src_x = R.dr.x1_save[m_ * 128:(m_ + 1) * 128, half * cfg.TH:(half + 1) * cfg.TH]
            S.add("sp", lambda E, d=sb.xin2[half][sl], s=src_x, m=tx[0]:
                  E.dma_start(out=d[:, :], in_=s).then_inc(m, 16))
            if half == 0:
                S.wait_tok("pe", twp)
                if m_ == 0:
                    S.wait_tok("pe", ag_toks[0])
                    S.wait_tok("pe", ag_toks[1])
            if m_ >= 2:
                S.wait("pe", R.c_act.sem, act_e[half][m_ - 2])
            for c in range(cfg.C):
                if c == cfg.C - 1:
                    pe_m[half][m_] = R.c_pe.tag()
                    S.add("pe", lambda E, p=pm[half][sl], h_=half, c_=c, w=sb.wp2[sl],
                          m=R.c_pe.sem: E.matmul(
                        p[:, :], w[:, c_, :], sb.agh[h_][:, c_, :],
                        start=False, stop=True).then_inc(m, 1))
                else:
                    S.add("pe", lambda E, p=pm[half][sl], h_=half, c_=c, w=sb.wp2[sl],
                          st=(c == 0): E.matmul(
                        p[:, :], w[:, c_, :], sb.agh[h_][:, c_, :], start=st, stop=False))
            S.wait("act", R.c_pe.sem, pe_m[half][m_])
            if m_ >= 2:
                S.wait("act", R.c_dve.sem, dve_m[half][m_ - 2])
            act_e[half][m_] = R.c_act.tag()
            S.add("act", lambda E, p=pm[half][sl], h_=half, m2=m_, sl_=sl, m=R.c_act.sem:
                  E.activation(sb.tmpd2[h_][sl_][:, :], p[:, :], AF.Identity,
                               bias=sb.bproj[:, m2:m2 + 1], scale=1.0).then_inc(m, 1))
            S.wait("dve", R.c_act.sem, act_e[half][m_])
            S.wait_tok("dve", tx)
            if m_ >= 2:
                S.wait_tok("dve", st_tok[half][m_ - 2])
            dve_m[half][m_] = R.c_dve.tag()
            S.add("dve", lambda E, h_=half, sl_=sl, m=R.c_dve.sem: E.tensor_tensor(
                sb.x2stage[h_][sl_][:, :], sb.tmpd2[h_][sl_][:, :],
                sb.xin2[h_][sl_][:, :], ALU.add).then_inc(m, 1))
            S.wait("sp", R.c_dve.sem, dve_m[half][m_])
            stok = ch_st.tag(sl)
            st_tok[half][m_] = stok
            dst = R.dr.x2_save[m_ * 128:(m_ + 1) * 128, half * cfg.TH:(half + 1) * cfg.TH]
            S.add("sp", lambda E, d=dst, h_=half, sl_=sl, m=stok[0]:
                  E.dma_start(out=d, in_=sb.x2stage[h_][sl_][:, :]).then_inc(m, 16))
    prev = [(R.c_dve.sem, dve_m[1][cfg.C - 2]), (R.c_dve.sem, dve_m[1][cfg.C - 1])]
    sqp = ()
    for half in range(2):
        R.sb.xin = sb.xin2[half]
        R.sb.sq = sb.sq2[half]
        pass_prev = list(prev) + [(R.c_dve.sem, dve_m[half][cfg.C - 2]),
                                  (R.c_dve.sem, dve_m[half][cfg.C - 1])]
        sq_pass = list(sqp) + [(R.c_dve.sem, dve_m[half][cfg.C - 2]),
                               (R.c_dve.sem, dve_m[half][cfg.C - 1]),
                               (R.c_act.sem, act_e[half][cfg.C - 1])]
        stats, prev, sqp = emit_ln_stats_stream(
            S, R, R.dr.x2_save, half, store_toks=st_tok[half],
            prev_waits=pass_prev, sq_prev=sq_pass,
            psums=(R.pstat[2 * half], R.pstat[2 * half + 1]))
        R.ab3_ready[half] = emit_ln_rows_and_bcast(
            S, R, R.pstat[2 * half], R.pstat[2 * half + 1], stats,
            (None, None), scr_rows=2 + 2 * half, defer=True)



def build_nc(cfg):
    nc = bass.Bass("TRN2", target_bir_lowering=False, detect_race_conditions=False,
                   dynamic_dma_scratch_size=16384)
    D, I, S_, TPC, TH, C, IC = cfg.D, cfg.I, cfg.S, cfg.TPC, cfg.TH, cfg.C, cfg.IC

    dr = SimpleNamespace()

    def din(name, shape, dt):
        return nc.dram_tensor(name, shape, dt, kind="ExternalInput").ap()

    dr.x_fm = din("x_fm", [D, TPC], F32)
    dr.wg_hb = din("wg_hb", [IC, 128, C, 128], I32)
    dr.wu_hb = din("wu_hb", [IC, 128, C, 128], I32)
    dr.wd_hb = din("wd_hb", [C, 128, IC, 128], I32)
    dr.wqkv = din("wqkv", [D, cfg.HPC * 384], F32)
    dr.bq_d = din("bqkv", [128, 3 * cfg.HPC], F32)
    dr.wproj = din("wproj", [D, D], F32)
    dr.bproj_d = din("bproj2", [128, C], F32)
    dr.scl = din("scl", [1, 3], F32)
    dr.cvals = din("cvals", [1, 2], F32)
    dr.cos_t = din("cos_t", [128, S_], F32)
    dr.sin_t = din("sin_t", [128, S_], F32)
    dr.ident = din("ident", [128, 128], BF16)
    dr.mask01 = din("mask01", [128, 128], BF16)
    dr.blkoff = din("blkoff", [1, 1], I32)
    dbg_kind = "ExternalOutput" if getattr(cfg, "debug", False) else "Internal"
    dr.out_fm = nc.dram_tensor("out_fm", [D, TPC], F32, kind="ExternalOutput").ap()
    dr.x1_save = nc.dram_tensor("x1_save", [D, TPC], F32, kind=dbg_kind).ap()
    dr.bc_scr_t = nc.dram_tensor("bc_scr", [6, TH], F32)
    dr.bc_scr = dr.bc_scr_t.ap()
    dr.x2_save = nc.dram_tensor("x2_save", [D, TPC], F32, kind=dbg_kind).ap()
    dr.y2_cc_h = nc.dram_tensor("y2_cc", [D, TPC], BF16)
    dr.y2_cc_ap = dr.y2_cc_h.ap()
    dr.y2_gath_h = nc.dram_tensor("y2_gath", [4 * D, TPC], BF16)
    dr.y2_gath = dr.y2_gath_h.ap()
    dr.ao_cc_h = nc.dram_tensor("ao_cc", [cfg.HPC * 128, S_], BF16)
    dr.ao_cc = dr.ao_cc_h.ap()
    dr.ao_gath_h = nc.dram_tensor("ao_gath", [4 * cfg.HPC * 128, S_], BF16)
    if getattr(cfg, "debug", False):
        dr.dbg_y2cc = nc.dram_tensor("y2_cc_dbg", [4 * D, TPC], BF16, kind="ExternalOutput").ap()
        dr.dbg_aocc = nc.dram_tensor("ao_cc_dbg", [4 * cfg.HPC * 128, S_], BF16, kind="ExternalOutput").ap()
        dr.dbg_aoctr = nc.dram_tensor("ao_ctr_dbg", [cfg.HPC * 128, S_], BF16, kind="ExternalOutput").ap()
        dr.dbg_y = nc.dram_tensor("dbg_y", [128, C, TH], BF16).ap()
        dr.dbg_rows = nc.dram_tensor("dbg_rows", [1, 2 * TH], F32, kind="ExternalOutput").ap()
        dr.dbg_rows2 = nc.dram_tensor("dbg_rows2", [1, 2 * TH], F32, kind="ExternalOutput").ap()
        dr.dbg_ab = nc.dram_tensor("dbg_ab", [128, 2 * TH], F32, kind="ExternalOutput").ap()
        dr.dbg_h = nc.dram_tensor("dbg_h", [128, IC, TH], BF16).ap()
        dr.dbg_q = nc.dram_tensor("dbg_q", [128, S_], BF16, kind="ExternalOutput").ap()
        dr.dbg_k = nc.dram_tensor("dbg_k", [128, S_], BF16, kind="ExternalOutput").ap()
        dr.dbg_v = nc.dram_tensor("dbg_v", [128, S_], BF16, kind="ExternalOutput").ap()
        dr.dbg_p = nc.dram_tensor("dbg_p", [128, S_], F32, kind="ExternalOutput").ap()

    with ExitStack() as ctx:
        R = SimpleNamespace()
        R.cfg, R.dr = cfg, dr
        R.c_pe = Ctr(nc, ctx, "cpe")
        R.c_dve = Ctr(nc, ctx, "cdve")
        R.c_act = Ctr(nc, ctx, "cact")
        R.c_cc = Ctr(nc, ctx, "ccc")
        R.ch_const = Chan(nc, ctx, "chc", 1)
        R.ch_x = Chan(nc, ctx, "chx", 2)
        R.ch_x1r = Chan(nc, ctx, "chxr", 2)
        R.ch_wg = Chan(nc, ctx, "chwg", 2)
        R.ch_wu = Chan(nc, ctx, "chwu", 2)
        R.ch_wd = Chan(nc, ctx, "chwd", 2)
        R.ch_sx1 = Chan(nc, ctx, "chs1", 2)
        R.ch_sx2 = Chan(nc, ctx, "chs2", 2)
        R.ch_y2w = Chan(nc, ctx, "chyw", 1)
        R.ch_y2r = Chan(nc, ctx, "chyr", 2)
        R.ch_wq = Chan(nc, ctx, "chwq", 2)
        R.ch_ao = Chan(nc, ctx, "chao", 1)
        R.ch_ag = Chan(nc, ctx, "chag", 2)
        R.ch_wp = Chan(nc, ctx, "chwp", 2)
        R.ch_dbg = Chan(nc, ctx, "chdbg", 1)
        R.ch_cs = Chan(nc, ctx, "chcs", 1)
        R.ch_bcw = Chan(nc, ctx, "chbw", 2)
        R.ch_bcr = Chan(nc, ctx, "chbr", 2)

        sb = SimpleNamespace()
        R.sb = sb
        sb.ones_col = ctx.enter_context(nc.sbuf_tensor("ones_col", [128, 1], F32))
        sb.ones_row = ctx.enter_context(nc.sbuf_tensor("ones_row", [1, 128], F32))
        sb.scl_row = ctx.enter_context(nc.sbuf_tensor("scl_row", [1, 3], F32))
        sb.cvals = ctx.enter_context(nc.sbuf_tensor("cvals_sb", [1, 2], F32))
        nc.const_aps.register(F32, EPS, sb.cvals[0:1, 0:1])
        nc.const_aps.register(F32, QSC, sb.cvals[0:1, 1:2])
        sb.scl_b = ctx.enter_context(nc.sbuf_tensor("scl_b", [128, 4], F32))
        sb.ident = ctx.enter_context(nc.sbuf_tensor("identt", [128, 128], BF16))
        sb.mask01 = ctx.enter_context(nc.sbuf_tensor("mask01t", [128, 128], BF16))
        sb.bq = ctx.enter_context(nc.sbuf_tensor("bqt", [128, 3 * cfg.HPC], F32))
        sb.bqs = ctx.enter_context(nc.sbuf_tensor("bqst", [128, 2], F32))
        sb.bproj = ctx.enter_context(nc.sbuf_tensor("bprojt", [128, C], F32))
        sb.blkoff = ctx.enter_context(nc.sbuf_tensor("blkofft", [1, 1], I32))
        sb.rows = [ctx.enter_context(nc.sbuf_tensor(f"rows{i}", [1, TH], F32)) for i in range(4)]
        if getattr(cfg, "debug", False):
            sb.dbg_rows_sb = ctx.enter_context(nc.sbuf_tensor("dbg_rows_sb", [1, TH], F32))

        R.y2_store_toks = []
        R.bc_prev = None
        R.ab_free = None
        R.ao_store_toks = []
        R.ab3_ready = [None, None]
        R.wq_free = [None, None]
        R.y2_cons = {}
        R.tr_gg = 0
        R.tr_hist = []

        with nc.psum_tensor("pinit", [128, 8], F32) as ps0:
            R.ps0 = ps0
            S = Sched()
            emit_init(S, R)
            run_block(nc, S)

        def ffn_scope(fctx, tag):
            sb.y = fctx.enter_context(nc.sbuf_tensor(f"y{tag}", [128, C, TH], BF16))
            sb.h = fctx.enter_context(nc.sbuf_tensor(f"h{tag}", [128, IC, TH], BF16))
            sb.xin = [fctx.enter_context(nc.sbuf_tensor(f"xin{tag}{i}", [128, TH], F32)) for i in range(2)]
            sb.tmpd = [fctx.enter_context(nc.sbuf_tensor(f"tmpd{tag}{i}", [128, TH], F32)) for i in range(2)]
            sb.tmp = sb.tmpd[0]
            sb.x1stage = [fctx.enter_context(nc.sbuf_tensor(f"x1s{tag}{i}", [128, TH], F32)) for i in range(2)]
            sb.ustage = [fctx.enter_context(nc.sbuf_tensor(f"ust{tag}{i}", [128, TH], BF16)) for i in range(2)]
            sb.wg2 = [fctx.enter_context(nc.sbuf_tensor(f"wg2{tag}{i}", [128, C, 128], BF16)) for i in range(2)]
            sb.wu2 = [fctx.enter_context(nc.sbuf_tensor(f"wu2{tag}{i}", [128, C, 128], BF16)) for i in range(2)]
            sb.wd2 = [fctx.enter_context(nc.sbuf_tensor(f"wd2{tag}{i}", [128, cfg.ICH, 128], BF16)) for i in range(2)]
            sb.sq = sb.tmpd
            sb.A = sb.x1stage[0]
            sb.B = sb.x1stage[1]

        with ExitStack() as fctx:
            ffn_scope(fctx, "a")
            with (nc.psum_tensor("pf0", [128, TH], F32) as ps0,
                  nc.psum_tensor("pf1", [128, TH], F32) as ps1,
                  nc.psum_tensor("pf2", [128, TH], F32) as ps2,
                  nc.psum_tensor("pf3", [128, TH], F32) as ps3):
                R.ps0, R.ps1, R.ps2, R.ps3 = ps0, ps1, ps2, ps3
                for half in range(2):
                    S = Sched()
                    stats, xin_free, sq_free = emit_ln_stats_stream(S, R, dr.x_fm, half)
                    ab_ready = emit_ln_rows_and_bcast(S, R, R.ps2, R.ps3, stats, (sb.A, sb.B))
                    y_ready, norm_free = emit_normalize_stream(
                        S, R, dr.x_fm, half, (sb.A, sb.B), ab_ready, prev_waits=xin_free)
                    if getattr(cfg, "debug", False) and half == 0:
                        emit_debug_dump(S, R, sb.y[:, :, :], dr.dbg_y, R.c_dve, y_ready)
                    emit_ffn(S, R, app=1, half=half, y_ready=y_ready,
                             xin_prev=norm_free, sq_prev=sq_free)
                    if getattr(cfg, "debug", False) and half == 0:
                        emit_debug_dump(S, R, sb.h[:, :, :], dr.dbg_h, R.c_dve, R.c_dve.n)
                    run_block(nc, S)

        with ExitStack() as actx:
            sb.cos_t = actx.enter_context(nc.sbuf_tensor("cos_tt", [128, S_], F32))
            sb.sin_t = actx.enter_context(nc.sbuf_tensor("sin_tt", [128, S_], F32))
            sb.wq = actx.enter_context(nc.sbuf_tensor("wq", [128, C, 768], BF16))
            sb.Qa = actx.enter_context(nc.sbuf_tensor("Qa", [128, S_], BF16))
            sb.Qb = actx.enter_context(nc.sbuf_tensor("Qb", [128, S_], BF16))
            sb.Ka = actx.enter_context(nc.sbuf_tensor("Ka", [128, S_], BF16))
            sb.Kb = actx.enter_context(nc.sbuf_tensor("Kb", [128, S_], BF16))
            sb.Va = actx.enter_context(nc.sbuf_tensor("Vaa", [128, S_], BF16))
            sb.Vb = actx.enter_context(nc.sbuf_tensor("Vbb", [128, S_], BF16))
            sb.y2sb = [actx.enter_context(nc.sbuf_tensor(f"y2sb{i}", [128, cfg.CG, TH], BF16))
                       for i in range(2)]
            sb.qf = actx.enter_context(nc.sbuf_tensor("qf", [128, TH], F32))
            sb.rot = actx.enter_context(nc.sbuf_tensor("rotb", [128, TH], F32))
            sb.t1 = actx.enter_context(nc.sbuf_tensor("t1b", [128, TH], F32))
            sb.t2 = actx.enter_context(nc.sbuf_tensor("t2b", [128, TH], F32))
            sb.vstage = actx.enter_context(nc.sbuf_tensor("vstage", [128, TH], BF16))
            sb.scores = actx.enter_context(nc.sbuf_tensor("scores", [128, S_], F32))
            sb.P = [actx.enter_context(nc.sbuf_tensor(f"Pm{i}", [128, S_], BF16)) for i in range(2)]
            sb.PT = [actx.enter_context(nc.sbuf_tensor(f"PTs{i}", [128, 512], BF16)) for i in range(2)]
            sb.outT = actx.enter_context(nc.sbuf_tensor("outTt", [128, S_], BF16))
            sb.negmax = actx.enter_context(nc.sbuf_tensor("negmax", [128, 1], F32))
            sb.lrow = actx.enter_context(nc.sbuf_tensor("lrowt", [128, 1], F32))
            sb.rrow = actx.enter_context(nc.sbuf_tensor("rrowt", [128, 1], F32))
            with (nc.psum_tensor("pa0", [128, 512], F32) as a0,
                  nc.psum_tensor("pa1", [128, 512], F32) as a1,
                  nc.psum_tensor("pa2", [128, 512], F32) as a2,
                  nc.psum_tensor("pa3", [128, 512], F32) as a3,
                  nc.psum_tensor("pa4", [128, 512], F32) as a4,
                  nc.psum_tensor("pa5", [128, 512], F32) as a5,
                  nc.psum_tensor("pat0", [128, 1024], BF16) as at0,
                  nc.psum_tensor("pat1", [128, 1024], BF16) as at1):
                R.aq = [a0, a1, a2, a3, a4, a5]
                R.atr = [at0, at1]
                S = Sched()

                def _hook(S2):
                    R.cc1_tok = emit_collective(S2, R, 1)
                R.coll1_hook = _hook
                emit_attention(S, R)
                run_block(nc, S)

        with ExitStack() as pctx:
            sb.agh = [pctx.enter_context(nc.sbuf_tensor(f"agh{i}", [128, C, TH], BF16))
                      for i in range(2)]
            sb.wp2 = [pctx.enter_context(nc.sbuf_tensor(f"wp2_{i}", [128, C, 128], BF16))
                      for i in range(2)]
            sb.xin2 = [[pctx.enter_context(nc.sbuf_tensor(f"xin2_{h}_{i}", [128, TH], F32))
                        for i in range(2)] for h in range(2)]
            sb.tmpd2 = [[pctx.enter_context(nc.sbuf_tensor(f"tmpd2_{h}_{i}", [128, TH], F32))
                         for i in range(2)] for h in range(2)]
            sb.x2stage = [[pctx.enter_context(nc.sbuf_tensor(f"x2s_{h}_{i}", [128, TH], F32))
                           for i in range(2)] for h in range(2)]
            sb.sq2 = sb.tmpd2
            with (nc.psum_tensor("pp0", [128, TH], F32) as ps0,
                  nc.psum_tensor("pp1", [128, TH], F32) as ps1,
                  nc.psum_tensor("pp2", [128, TH], F32) as ps2,
                  nc.psum_tensor("pp3", [128, TH], F32) as ps3,
                  nc.psum_tensor("pq0", [128, TH], F32) as pt0,
                  nc.psum_tensor("pq1", [128, TH], F32) as pt1,
                  nc.psum_tensor("pq2", [128, TH], F32) as pt2,
                  nc.psum_tensor("pq3", [128, TH], F32) as pt3):
                R.ps0, R.ps1, R.ps2, R.ps3 = ps0, ps1, ps2, ps3
                R.pstat = [pt0, pt1, pt2, pt3]
                S = Sched()
                emit_proj(S, R)
                run_block(nc, S)

        with ExitStack() as fctx:
            ffn_scope(fctx, "b")
            with (nc.psum_tensor("pg0", [128, TH], F32) as ps0,
                  nc.psum_tensor("pg1", [128, TH], F32) as ps1,
                  nc.psum_tensor("pg2", [128, TH], F32) as ps2,
                  nc.psum_tensor("pg3", [128, TH], F32) as ps3):
                R.ps0, R.ps1, R.ps2, R.ps3 = ps0, ps1, ps2, ps3
                for half in range(2):
                    S = Sched()
                    for t in R.ab3_ready[half]:
                        S.wait_tok("sp", t)
                    ab_ready = emit_ab_load(S, R, 2 + 2 * half, sb.A, sb.B)
                    y_ready, norm_free = emit_normalize_stream(
                        S, R, dr.x2_save, half, (sb.A, sb.B), ab_ready)
                    emit_ffn(S, R, app=2, half=half, y_ready=y_ready,
                             xin_prev=norm_free)
                    run_block(nc, S)

    return nc



def host_prepare(cfg, x, w_attn, b_attn, w_proj, b_proj, gate_w, gate_s, up_w, up_s,
                 down_w, down_s):
    D, I, S_, TPC, C, IC, HPC = cfg.D, cfg.I, cfg.S, cfg.TPC, cfg.C, cfg.IC, cfg.HPC
    x = np.asarray(x, dtype=np.float32)
    w_attn = np.asarray(w_attn, dtype=np.float32)
    b_attn = np.asarray(b_attn, dtype=np.float32)
    w_proj = np.ascontiguousarray(np.asarray(w_proj, dtype=np.float32))
    b_proj = np.asarray(b_proj, dtype=np.float32)
    gate_w = np.asarray(gate_w, dtype=np.int32)
    up_w = np.asarray(up_w, dtype=np.int32)
    down_w = np.asarray(down_w, dtype=np.int32)

    wg_hb = np.ascontiguousarray(gate_w.reshape(IC, 128, C, 128).transpose(0, 3, 2, 1))
    wu_hb = np.ascontiguousarray(up_w.reshape(IC, 128, C, 128).transpose(0, 3, 2, 1))
    wd_hb = np.ascontiguousarray(down_w.reshape(C, 128, IC, 128).transpose(0, 3, 2, 1))
    scl = np.array([[float(np.asarray(gate_s).reshape(-1)[0]),
                     float(np.asarray(up_s).reshape(-1)[0]),
                     float(np.asarray(down_s).reshape(-1)[0])]], dtype=np.float32)
    bproj2 = np.ascontiguousarray(b_proj.reshape(C, 128).T)

    inv = 1.0 / (10000.0 ** (np.arange(0, 128, 2, dtype=np.float64) / 128.0))
    pos = np.arange(S_, dtype=np.float64)
    ang = inv[:, None] * pos[None, :]
    cos_t = np.ascontiguousarray(np.concatenate([np.cos(ang), np.cos(ang)], 0).astype(np.float32))
    sin_t = np.ascontiguousarray(np.concatenate([-np.sin(ang), np.sin(ang)], 0).astype(np.float32))
    ident = np.eye(128, dtype=np.float32).astype(ml_dtypes.bfloat16)
    mask01 = np.tril(np.ones((128, 128), dtype=np.float32)).astype(ml_dtypes.bfloat16)

    in_maps = []
    for r in range(8):
        s, b = r // 4, r % 4
        x_fm = np.ascontiguousarray(x[s, b * TPC:(b + 1) * TPC, :].T)
        h0 = HPC * b
        cols, bcols = [], []
        for hh in range(HPC):
            h = h0 + hh
            for part in range(3):
                cols.append(w_attn[:, part * D + h * 128:part * D + (h + 1) * 128])
                bcols.append(b_attn[part * D + h * 128:part * D + (h + 1) * 128])
        wqkv = np.ascontiguousarray(np.concatenate(cols, axis=1))
        bqkv = np.ascontiguousarray(np.stack(bcols, axis=1))
        in_maps.append(dict(
            x_fm=x_fm, wg_hb=wg_hb, wu_hb=wu_hb, wd_hb=wd_hb, wqkv=wqkv, bqkv=bqkv,
            wproj=w_proj, bproj2=bproj2, scl=scl, cos_t=cos_t, sin_t=sin_t,
            ident=ident, mask01=mask01,
            cvals=np.array([[EPS, QSC]], dtype=np.float32),
            blkoff=np.array([[b * TPC]], dtype=np.int32),
        ))
    return in_maps


def host_assemble(cfg, results):
    out = np.empty((2, cfg.S, cfg.D), dtype=np.float32)
    for r in range(8):
        s, b = r // 4, r % 4
        out[s, b * cfg.TPC:(b + 1) * cfg.TPC, :] = np.asarray(results[r]["out_fm"]).T
    return out


def kernel(L0_attn_hidden_states, L0_attn_w_attn, L0_attn_b_attn, L0_attn_w_proj,
           L0_attn_b_proj, L0_ffn_gate_w, L0_ffn_gate_s, L0_ffn_up_w, L0_ffn_up_s,
           L0_ffn_down_w, L0_ffn_down_s):
    cfg = FULL
    in_maps = host_prepare(cfg, L0_attn_hidden_states, L0_attn_w_attn, L0_attn_b_attn,
                           L0_attn_w_proj, L0_attn_b_proj, L0_ffn_gate_w, L0_ffn_gate_s,
                           L0_ffn_up_w, L0_ffn_up_s, L0_ffn_down_w, L0_ffn_down_s)
    nc = build_nc(cfg)
    res = run_bass_kernel_spmd(nc, in_maps, core_ids=list(range(8)))
    return host_assemble(cfg, res.results)
